# version 17
# baseline (speedup 1.0000x reference)
import numpy as np
import concourse.bass as bass
import concourse.mybir as mybir
from concourse.bass_utils import run_bass_kernel_spmd

F32 = mybir.dt.float32
BF16 = mybir.dt.bfloat16
ALU = mybir.AluOpType
AF = mybir.ActivationFunctionType
AX = mybir.AxisListType.X

D = 2048
ND = 16
II = 5632
NI = 44
NIH = 22
T = 8192
TC = 1024
NCORE = 8
TD = float(T * D)
EPS = 1e-5

TAPS = ()
LIMIT = 99
NO_CC = False
NDA = 16
DBG0 = False

KB = 1024
X0 = 0
Y0 = 64 * KB
Z0 = 128 * KB
ARENA_BYTES = 168 * KB


class Sem:

    def __init__(self, name):
        self.name = name
        self.h = None
        self.val = 0
        self.marks = {}

    def reset(self):
        self.val = 0

    def inc(self, tag=None, n=1):
        self.val += n
        if tag is not None:
            self.marks[tag] = self.val
        return self.val

    def at(self, tag):
        return self.marks[tag]


class Builder:
    def __init__(self):
        nc = bass.Bass("TRN2", target_bir_lowering=False, debug=False)
        self.nc = nc
        self.x_h = nc.dram_tensor("x", [D, TC], F32, kind="ExternalInput").ap()
        self.wr_h = nc.dram_tensor("wr", [D, D], F32, kind="ExternalInput").ap()
        self.wk_h = nc.dram_tensor("wk", [D, D], F32, kind="ExternalInput").ap()
        self.wv_h = nc.dram_tensor("wv", [D, D], F32, kind="ExternalInput").ap()
        self.wo_h = nc.dram_tensor("wo", [D, D], F32, kind="ExternalInput").ap()
        self.gw_h = nc.dram_tensor("gw", [II, D], F32, kind="ExternalInput").ap()
        self.uw_h = nc.dram_tensor("uw", [II, D], F32, kind="ExternalInput").ap()
        self.dw_h = nc.dram_tensor("dw", [D, II], F32, kind="ExternalInput").ap()
        self.vecs16_h = nc.dram_tensor("vecs16", [128, 64], F32, kind="ExternalInput").ap()
        self.vecs44_h = nc.dram_tensor("vecs44", [128, 88], F32, kind="ExternalInput").ap()
        self.iota_h = nc.dram_tensor("iota", [128, TC], F32, kind="ExternalInput").ap()
        self.cmask_h = nc.dram_tensor("cmask", [128, 32], F32, kind="ExternalInput").ap()
        self.y_h = nc.dram_tensor("y", [D, TC], F32, kind="ExternalOutput").ap()

        self.h0_d = nc.dram_tensor("h0d", [D, TC], F32)
        self.h1_d = nc.dram_tensor("h1d", [D, TC], F32)
        self.cc_in = [nc.dram_tensor(f"ccin{g}", [128, 4], F32) for g in range(4)]
        self.cc_out = [nc.dram_tensor(f"ccout{g}", [128 * NCORE, 4], F32) for g in range(4)]
        self.st_in = nc.dram_tensor("stin", [1, 2], F32)
        self.st_out = nc.dram_tensor("stout", [NCORE, 2], F32)

        self.taps = {}
        for t in TAPS:
            shp, dt_ = {
                "p1": ([128, ND * TC], BF16),
                "p2": ([128, ND * TC], F32),
                "p3": ([128, ND * TC], BF16),
                "p4h": ([128, ND * TC], F32),
                "p4z": ([128, ND * TC], F32),
                "p4w": ([128, 32], F32),
                "p4b": ([128, 16], F32),
                "p4f": ([128, ND * TC], BF16),
                "p5a": ([128, NIH * TC], BF16),
                "p5b": ([128, ND * TC], F32),
                "p6": ([128, ND * TC], F32),
            }[t]
            self.taps[t] = nc.dram_tensor("tap_" + t, shp, dt_, kind="ExternalOutput").ap()
        if DBG0:
            self.att0 = nc.dram_tensor("att0", [128, 4 * TC], F32, kind="ExternalOutput").ap()

        self.sT = Sem("sT")
        self.sV = Sem("sV")
        self.sC = Sem("sC")
        self.sCC = Sem("sCC")
        names = ["cst", "gts", "gtb", "gn", "xsp", "h1sp", "tapq",
                 "xld0", "xld1", "xld2", "xld3",
                 "ct0", "ct1", "ct2", "ct3"]
        for pn in ("cw", "co", "cg", "cd", "h0", "h1r", "yq"):
            names += [pn + "0", pn + "1"]
        self.dsem = {n: Sem(n) for n in names}


    def ps(self, base, idx):
        return self.dsem[base + str(idx % 2)]

    def cut(self, w, e, n, sem, tag, view):
        if LIMIT > n:
            return False
        if w == "s":
            self.W(e, sem, tag)
            self.I(lambda: e.dma_start(
                out=self.y_h.rearrange("(c p) t -> p c t", p=128), in_=view),
                self.dsem["yq0"], "ycut", 16)
            self.W(e, self.dsem["yq0"], "ycut")
        return True
    def carve(self, off, p, shape_free_elems, dt):
        esz = 2 if dt == BF16 else 4
        nb = int(np.prod(shape_free_elems)) * esz
        assert off + nb <= ARENA_BYTES, (off, nb)
        ap = self.arena.ap()[0:p, off:off + nb].bitcast(dt)
        if len(shape_free_elems) == 2:
            a, b = shape_free_elems
            ap = ap.rearrange("p (a b) -> p a b", b=b)
        return ap

    def build(self):
        nc = self.nc
        import contextlib
        es = contextlib.ExitStack()
        self.arena = es.enter_context(nc.sbuf_tensor("arena", [128, ARENA_BYTES], mybir.dt.uint8))
        self.abuf = es.enter_context(nc.sbuf_tensor("abuf", [128, ND * TC], BF16))
        self.iota = es.enter_context(nc.sbuf_tensor("iota_s", [128, TC], F32))
        self.vecs16 = es.enter_context(nc.sbuf_tensor("v16", [128, 64], F32))
        self.vecs44 = es.enter_context(nc.sbuf_tensor("v44", [128, 88], F32))
        self.ldc = es.enter_context(nc.sbuf_tensor("ldc", [128, 16], F32))
        self.nldc = es.enter_context(nc.sbuf_tensor("nldc", [128, 16], F32))
        self.dnsb = es.enter_context(nc.sbuf_tensor("dnsb", [128, 16], BF16))
        self.gsb = es.enter_context(nc.sbuf_tensor("gsb", [128, 64], F32))
        self.cmask = es.enter_context(nc.sbuf_tensor("cmask_s", [128, 32], F32))
        self.S1 = es.enter_context(nc.sbuf_tensor("S1", [128, 16], F32))
        self.S2 = es.enter_context(nc.sbuf_tensor("S2", [128, 16], F32))
        self.gt = es.enter_context(nc.sbuf_tensor("gt", [128, 4], F32))
        self.cin = es.enter_context(nc.sbuf_tensor("cin", [128, 4], F32))
        self.gath = es.enter_context(nc.sbuf_tensor("gath", [128, 32], F32))
        self.mm_sc = es.enter_context(nc.sbuf_tensor("mmsc", [128, 32], F32))
        self.onesb = es.enter_context(nc.sbuf_tensor("onesb", [128, 1], BF16))
        self.onesf = es.enter_context(nc.sbuf_tensor("onesf", [128, 1], F32))
        self.ones1r = es.enter_context(nc.sbuf_tensor("ones1r", [1, 128], F32))
        self.strow = es.enter_context(nc.sbuf_tensor("strow", [1, 2], F32))
        self.stg = es.enter_context(nc.sbuf_tensor("stg", [1, 16], F32))
        self.tot2 = es.enter_context(nc.sbuf_tensor("tot2", [1, 8], F32))
        self.mr_bc = es.enter_context(nc.sbuf_tensor("mrbc", [128, 2], F32))
        self.Agn = es.enter_context(nc.sbuf_tensor("Agn", [128, 16], F32))
        self.Bgn = es.enter_context(nc.sbuf_tensor("Bgn", [128, 16], F32))
        self.pk2 = es.enter_context(nc.sbuf_tensor("pk2", [128, 2], F32))
        self.gbw = es.enter_context(nc.sbuf_tensor("gbw", [128, 32], BF16))
        self.wgwsb = es.enter_context(nc.sbuf_tensor("wgwsb", [128, 32], F32))
        self.w2sb = es.enter_context(nc.sbuf_tensor("w2sb", [128, 16], F32))
        self.s2c = es.enter_context(nc.sbuf_tensor("s2c", [128, 1], F32))

        self.PA = es.enter_context(nc.psum_tensor("PA", [128, 1024], F32))
        self.PB = es.enter_context(nc.psum_tensor("PB", [128, 1024], F32))
        self.PC = es.enter_context(nc.psum_tensor("PC", [128, 1024], F32))
        self.PD = es.enter_context(nc.psum_tensor("PD", [128, 1024], F32))

        for s in self._all_sems():
            s.h = es.enter_context(nc.semaphore(s.name))

        cv = self.carve
        self.xres = cv(X0, 128, (ND, TC), F32)
        self.hdn1 = cv(X0, 128, (NIH, TC), BF16)
        pass
        self.z2bf = cv(X0 + 44 * KB, 128, (TC,), BF16)
        self.sqbf6 = cv(X0 + 46 * KB, 128, (TC,), BF16)
        self.rows6 = cv(X0 + 48 * KB, 1, (3 * TC,), F32)
        self.nmr_bc7 = cv(X0 + 60 * KB, 128, (TC,), F32)
        self.rstd_bc7 = cv(X0 + 44 * KB, 128, (TC,), F32)
        self.tmp7a = cv(X0 + 48 * KB, 128, (TC,), F32)
        self.tmp7b = cv(X0 + 52 * KB, 128, (TC,), F32)
        self.sqbf1 = cv(Z0 + 0 * KB, 128, (TC,), BF16)
        self.rows1 = cv(Z0 + 2 * KB, 1, (3 * TC,), F32)
        self.nmr_bc1 = cv(Z0 + 14 * KB, 128, (TC,), F32)
        self.rstd_bc1 = cv(Z0 + 18 * KB, 128, (TC,), F32)
        self.tmp1 = cv(Z0 + 22 * KB, 128, (TC,), F32)
        off = Y0 + 24 * KB
        self.trW = {}
        for w in ("r", "k", "v"):
            self.trW[w] = [cv(off + i * 4 * KB, 128, (ND, 128), BF16) for i in range(2)]
            off += 8 * KB
        self.kbuf = cv(Y0 + 48 * KB, 128, (TC,), F32)
        self.vbuf = cv(Y0 + 52 * KB, 128, (TC,), F32)
        self.scbuf = cv(Y0 + 56 * KB, 128, (TC,), F32)
        self.s0 = cv(Y0 + 60 * KB, 128, (TC,), F32)
        self.z1 = cv(Y0, 128, (ND, TC), F32)
        self.xbfv = [cv(X0 + c * 4 * KB, 128, (TC,), BF16) for c in range(ND)]
        self.trWoA = [cv(Y0 + 40 * KB, 128, (ND, 128), BF16), cv(Y0 + 44 * KB, 128, (ND, 128), BF16),
                      cv(Y0 + 60 * KB, 128, (ND, 128), BF16)]
        self.s1b = cv(Z0 + 0 * KB, 128, (TC,), F32)
        self.Bring = [cv(Z0 + 4 * KB + i * 4 * KB, 128, (TC,), F32) for i in range(8)]
        self.sqdump = cv(Z0 + 36 * KB, 128, (TC,), BF16)
        self.trWo = [cv(Z0 + 8 * KB + i * 4 * KB, 128, (ND, 128), BF16) for i in range(2)]
        self.trWoT = [cv(X0 + off * KB, 128, (4, 128), BF16)
                      for off in (54, 55, 58, 59)]
        self.trG = [cv(Z0 + 16 * KB + i * 4 * KB, 128, (ND, 128), BF16) for i in range(2)]
        self.trU = [cv(Z0 + 24 * KB + i * 4 * KB, 128, (ND, 128), BF16) for i in range(2)]
        self.sgB = cv(Z0 + 32 * KB, 128, (TC,), F32)
        self.uuB = cv(Z0 + 36 * KB, 128, (TC,), F32)
        self.trD = [cv(Z0 + 0 * KB + i * 6 * KB, 128, (NIH, 128), BF16) for i in range(2)]
        self.h1rl = [cv(Z0 + 24 * KB + i * 4 * KB, 128, (TC,), F32) for i in range(4)]
        self.outb = [cv(Z0 + 0 * KB + i * 4 * KB, 128, (TC,), F32) for i in range(4)]

        ab = self.abuf.ap()
        self.z1bfP = [ab[:, 14336:15360], ab[:, 15360:16384]]
        self.sqbfP = [ab[:, 12288:13312], ab[:, 13312:14336]]
        self.h0rl = [ab[:, 8192:10240].bitcast(F32), ab[:, 10240:12288].bitcast(F32)]
        self.tmpb = ab[:, 4096:6144].bitcast(F32)
        self.tmpa = cv(Z0 + 36 * KB, 128, (TC,), F32)
        self.rows4 = cv(X0 + 52 * KB, 1, (3 * TC,), F32)
        self.nmr_bc4 = cv(X0 + 44 * KB, 128, (TC,), F32)
        self.rstd_bc4 = cv(X0 + 48 * KB, 128, (TC,), F32)

        self.plan = True
        for w in ("g", "s", "a", "t", "v"):
            for s in self._all_sems():
                s.reset()
            self.stream(None, w)
        self.plan = False
        with nc.Block() as block:
            @block.gpsimd
            def _(g):
                self._emit(g, "g")

            @block.sync
            def _(s):
                self._emit(s, "s")

            @block.scalar
            def _(a):
                self._emit(a, "a")

            @block.tensor
            def _(t):
                self._emit(t, "t")

            @block.vector
            def _(v):
                self._emit(v, "v")
        es.close()
        return nc

    def _all_sems(self):
        return [self.sT, self.sV, self.sC, self.sCC] + list(self.dsem.values())

    def _emit(self, eng, which):
        for s in self._all_sems():
            s.reset()
        self.stream(eng, which)

    def I(self, fn, sem=None, tag=None, n=1):
        if not self.plan:
            with self.nc.named_scope(getattr(self, "phase", "p")):
                inst = fn()
            if sem is not None:
                inst.then_inc(sem.h, n)
        if sem is not None:
            sem.inc(tag, n)

    def W(self, eng, sem, tag):
        if self.plan:
            return
        eng.wait_ge(sem.h, sem.at(tag))

    def stream(self, e, w):
        PA, PB, PC, PD = self.PA, self.PB, self.PC, self.PD
        sT, sV, sC, sCC = self.sT, self.sV, self.sC, self.sCC
        ds = self.dsem
        dps = self.ps

        self.phase = "p0"
        if w in ("s", "g", "a"):
            for q in {"s": (0,), "g": (1,), "a": (2, 3)}[w]:
                self.I(lambda q=q: e.dma_start(
                    out=self.xres[:, 4 * q:4 * q + 4, :],
                    in_=self.x_h.rearrange("(c p) t -> p c t", p=128)[:, 4 * q:4 * q + 4, :]),
                    ds["xld" + str(q)], "xld", 16)
        if w == "g":
            self.I(lambda: e.dma_start(out=self.iota[:, :], in_=self.iota_h[:, :]), ds["cst"], None, 16)
            self.I(lambda: e.dma_start(out=self.vecs16[:, :], in_=self.vecs16_h[:, :]), ds["cst"], None, 16)
            self.I(lambda: e.dma_start(out=self.vecs44[:, :], in_=self.vecs44_h[:, :]), ds["cst"], None, 16)
            self.I(lambda: e.dma_start(out=self.cmask[:, :], in_=self.cmask_h[:, :]), ds["cst"], "cst", 16)
        if w == "a":
            self.W(e, ds["cst"], "cst")
            self.I(lambda: e.activation(out=self.gsb[:, 0:16], in_=self.vecs16[:, 0:16],
                                        func=AF.Exp, scale=-1.0), sC, "dexp")
            self.W(e, sV, "drec")
            self.I(lambda: e.activation(out=self.ldc[:, :], in_=self.gsb[:, 16:32],
                                        func=AF.Ln), sC, "dln")
        if w == "v":
            self.I(lambda: e.memset(self.onesb[:, :], 1.0), sV, "ob")
            self.I(lambda: e.memset(self.onesf[:, :], 1.0), sV, "of")
            self.I(lambda: e.memset(self.ones1r[:, :], 1.0), sV, "o1r")
            self.W(e, sC, "dexp")
            self.I(lambda: e.tensor_scalar_add(self.gsb[:, 16:32], self.gsb[:, 0:16], 1.0),
                   sV, "dadd")
            self.W(e, sV, "dadd")
            self.I(lambda: e.reciprocal(self.gsb[:, 32:48], self.gsb[:, 16:32]), sV, "drcp")
            self.W(e, sV, "drcp")
            self.I(lambda: e.tensor_scalar_max(self.gsb[:, 16:32], self.gsb[:, 32:48], 1e-7),
                   sV, "drec")
            self.W(e, sC, "dln")
            self.I(lambda: e.tensor_scalar_mul(self.nldc[:, :], self.ldc[:, :], -1.0), sV, "dnld")
            self.W(e, ds["cst"], "cst")
            self.I(lambda: e.tensor_copy(self.dnsb[:, :], self.vecs16[:, 48:64]), sV, "dnsb")
            self.I(lambda: e.tensor_copy(
                self.gbw[:, :].rearrange("p (d two) -> p d two", two=2)[:, :, 0:1],
                self.vecs16[:, 32:48].rearrange("p (d two) -> p d two", two=1)), sV, "gbw1")
            self.I(lambda: e.tensor_copy(
                self.gbw[:, :].rearrange("p (d two) -> p d two", two=2)[:, :, 1:2],
                self.vecs16[:, 16:32].rearrange("p (d two) -> p d two", two=1)), sV, "gbw")

        self.phase = "p1"
        if w == "v":
            for c in range(ND):
                if c % 4 == 0:
                    self.W(e, ds["xld" + str(c // 4)], "xld")
                self.I(lambda c=c: e.tensor_copy(self.abuf[:, c * TC:(c + 1) * TC],
                                                 self.xres[:, c, :]), sV, ("xbf", c))
        if w == "a":
            for c in range(ND):
                if c % 4 == 0:
                    self.W(e, ds["xld" + str(c // 4)], "xld")
                if c >= 1:
                    self.W(e, sT, ("st1", c - 1))
                self.I(lambda c=c: e.activation(out=self.sqbf1[:, :], in_=self.xres[:, c, :],
                                                func=AF.Square), sC, ("xsq", c))
        if w == "t":
            self.W(e, sV, "ob")
            for c in range(ND):
                self.W(e, sV, ("xbf", c))
                self.W(e, sC, ("xsq", c))
                for h in range(2):
                    self.I(lambda c=c, h=h: e.matmul(
                        PC[0:1, h * 512:(h + 1) * 512], self.onesb[:, 0:1],
                        self.abuf[:, c * TC + h * 512: c * TC + (h + 1) * 512],
                        start=(c == 0), stop=(c == ND - 1)))
                for h in range(2):
                    self.I(lambda c=c, h=h: e.matmul(
                        PD[0:1, h * 512:(h + 1) * 512], self.onesb[:, 0:1],
                        self.sqbf1[:, h * 512:(h + 1) * 512],
                        start=(c == 0), stop=(c == ND - 1)), sT, ("st1", c))
        self._ln_rows(w, e, PC, PD, self.rows1, self.nmr_bc1, self.rstd_bc1,
                      pref="ln1", wait_tag=("st1", ND - 1))
        if w == "v":
            self.W(e, sC, "ln1bc")
            for c in range(ND):
                self.I(lambda c=c: e.tensor_mul(self.tmp1[:, :], self.xres[:, c, :],
                                                self.rstd_bc1[:, :]))
                self.I(lambda c=c: e.tensor_add(self.abuf[:, c * TC:(c + 1) * TC],
                                                self.tmp1[:, :], self.nmr_bc1[:, :]),
                       sV, ("ain", c))
        
        if self.cut(w, e, 1, self.sV, ("ain", ND - 1), self.xres[:, :, :]):
            return

        self.phase = "p2"
        G = NDA // 4
        if w == "g":
            for c in range(NDA):
                par = c % 2
                if c >= 2:
                    self.W(e, sT, ("pe_r", c - 2))
                for wn, hsrc in (("r", self.wr_h), ("k", self.wk_h), ("v", self.wv_h)):
                    self.I(lambda wn=wn, hsrc=hsrc, c=c, par=par: e.dma_start(
                        out=self.trW[wn][par][:, :, :],
                        in_=hsrc[c * 128:(c + 1) * 128, :].rearrange(
                            "p (k j) -> p k j", j=128)), dps("cw", c),
                        ("cw", c) if wn == "v" else None, 16)
                if DBG0 and c == 0:
                    self.W(e, sC, ("vev", 0))
                    self.I(lambda: e.dma_start(out=self.att0[:, 0:TC], in_=self.kbuf[:, :]),
                           ds["tapq"], "d0a", 16)
                    self.I(lambda: e.dma_start(out=self.att0[:, TC:2 * TC], in_=self.vbuf[:, :]),
                           ds["tapq"], "d0b", 16)
                    self.W(e, sV, ("A", 0))
                    self.I(lambda: e.dma_start(out=self.att0[:, 2 * TC:3 * TC], in_=self.s1b[:, :]),
                           ds["tapq"], "d0c", 16)
                    self.I(lambda: e.dma_start(out=self.att0[:, 3 * TC:4 * TC], in_=self.Bring[0][:, :]),
                           ds["tapq"], "d0d", 16)
                for g in range(G):
                    if 4 * g + 5 == c:
                        self._carry_stage(e, g)
                    if 4 * g + 7 == c:
                        self._carry_back(e, g)
            for g in range(G):
                if 4 * g + 5 >= NDA:
                    self._carry_stage(e, g)
            for c in range(5):
                self._wo_load(e, c)
            for g in range(G):
                if 4 * g + 7 > NDA - 1:
                    self._carry_back(e, g)
        if w == "s":
            self.W(e, sV, ("ain", ND - 1))
            self.I(lambda: e.dma_start(
                out=self.h0_d.ap().rearrange("(c p) t -> p c t", p=128),
                in_=self.xres[:, :, :]), ds["xsp"], "xsp", 16)
            self._tap(e, "p1", self.abuf[:, :])
        if w == "t":
            for c in range(NDA):
                par = c % 2
                self.W(e, dps("cw", c), ("cw", c))
                for wn, ps in (("k", PA), ("v", PB), ("r", PC)):
                    if c >= 1:
                        self.W(e, sC, ({"k": "kev", "v": "vev", "r": "rexp"}[wn], c - 1))
                    for h in range(2):
                        for d in range(ND):
                            if c == 0 and wn == "k" and h == 0:
                                self.W(e, sV, ("ain", d))
                            last = (d == ND - 1 and h == 1)
                            self.I(lambda wn=wn, ps=ps, c=c, d=d, h=h, par=par: e.matmul(
                                ps[:, h * 512:(h + 1) * 512],
                                self.trW[wn][par][:, d, :],
                                self.abuf[:, d * TC + h * 512: d * TC + (h + 1) * 512],
                                start=(d == 0), stop=(d == ND - 1)),
                                sT if last else None, ("pe_" + wn, c) if last else None)
        if w == "a":
            self.W(e, ds["cst"], "cst")
            self.W(e, sV, "dnld")
            for c in range(NDA):
                par = c % 2
                self.W(e, sT, ("pe_k", c))
                if c >= 1:
                    self.W(e, sV, ("kvs", c - 1))
                self.I(lambda c=c: e.copy(out=self.kbuf[:, :], in_=PA[:, :]), sC, ("kev", c))
                self.W(e, sT, ("pe_v", c))
                self.I(lambda c=c: e.copy(out=self.vbuf[:, :], in_=PB[:, :]), sC, ("vev", c))
                self.W(e, sT, ("pe_r", c))
                if DBG0 and c == 1:
                    self.W(e, ds["tapq"], "d0d")
                if c >= 1:
                    self.W(e, sV, ("A", c - 1))
                self.I(lambda c=c: e.activation(out=self.s1b[:, :], in_=PC[:, :],
                                                func=AF.Tanh, scale=0.5), sC, ("rexp", c))
                self.I(lambda c=c: e.activation(out=self.scbuf[:, :], in_=self.iota[:, :],
                                                func=AF.Exp, scale=self.ldc[:, c:c + 1]),
                       sC, ("scl", c))
                self.W(e, sV, ("inva", c))
                self.I(lambda c=c: e.activation(out=self.s0[:, :], in_=self.s0[:, :],
                                                func=AF.Exp), sC, ("inv", c))
                glist = []
                if c % 4 == 3 and c >= 7 and c < NDA - 1:
                    glist.append(c // 4 - 1)
                if c == NDA - 1:
                    glist += [g for g in range(NDA // 4 - 1) if 4 * g + 7 >= NDA - 1]
                for gidx in glist:
                    for cc in range(gidx * 4, gidx * 4 + 4):
                        self.W(e, sV, ("corr", cc))
                        self.I(lambda cc=cc: e.activation(
                            out=self.sqdump[:, :], in_=self.xres[:, cc, :],
                            func=AF.Square, accum_out=self.S2[:, cc:cc + 1]),
                            sC, ("gnsq", cc))
        if w == "v":
            for c in range(NDA):
                bslot = c % 8
                if c == 0:
                    self.W(e, ds["cst"], "cst")
                if DBG0 and c == 1:
                    self.W(e, ds["tapq"], "d0d")
                self.W(e, sC, ("scl", c))
                self.I(lambda c=c, bslot=bslot: e.scalar_tensor_tensor(
                    self.Bring[bslot][:, :], self.s1b[:, :], 1.0, self.scbuf[:, :],
                    ALU.add, ALU.mult), sV, ("rr", c))
                self.I(lambda c=c: e.tensor_scalar(
                    self.s0[:, :], self.iota[:, :], self.nldc[:, c:c + 1], 23.0258509299,
                    ALU.mult, ALU.min), sV, ("inva", c))
                if DBG0 and c == 0:
                    self.W(e, ds["tapq"], "d0b")
                self.W(e, sC, ("inv", c))
                self.I(lambda c=c: e.tensor_mul(self.kbuf[:, :], self.kbuf[:, :],
                                                self.vbuf[:, :]), sV, ("kv", c))
                self.I(lambda c=c: e.scalar_tensor_tensor(
                    self.s0[:, :], self.s0[:, :], 1e10, self.kbuf[:, :],
                    ALU.min, ALU.mult), sV, ("kvs", c))
                self.I(lambda c=c: e.tensor_tensor_scan(
                    self.s1b[:, :], self.s0[:, :], self.s0[:, :], 0.0,
                    ALU.add, ALU.bypass), sV, ("scan", c))
                if self.plan:
                    self.sV.marks[("rs", c)] = self.sV.val
                if c == 0:
                    self.W(e, ds["xsp"], "xsp")
                self.I(lambda c=c, bslot=bslot: e.tensor_mul(
                    self.xres[:, c, :], self.s1b[:, :], self.Bring[bslot][:, :]),
                    sV, ("A", c))
                if c >= 4:
                    self.W(e, ds["gts"], ("gts", c // 4 - 1))
                self.I(lambda c=c: e.tensor_copy(self.gt[:, c % 4:c % 4 + 1],
                                                 self.s1b[:, 1023:1024]), sV, ("gt", c))
                glist = []
                if c % 4 == 3 and c >= 7 and c < NDA - 1:
                    glist.append(c // 4 - 1)
                if c == NDA - 1:
                    glist += [g for g in range(G) if 4 * g + 7 >= NDA - 1]
                for gidx in glist:
                    self.W(e, ds["gtb"], ("gtb", gidx))
                    if gidx == 0:
                        self.W(e, ds["cst"], "cst")
                    self.I(lambda: e.tensor_mul(self.mm_sc[:, :], self.gath[:, :],
                                                self.cmask[:, :]), sV, ("mmg", gidx))
                    self.W(e, sV, ("mmg", gidx))
                    self.I(lambda g=gidx: e.tensor_reduce(
                        self.cin[:, :],
                        self.mm_sc[:, :].rearrange("p (r j) -> p j r", j=4),
                        AX, ALU.add), sV, ("cin", gidx))
                    self.W(e, sV, ("cin", gidx))
                    for cc in range(gidx * 4, gidx * 4 + 4):
                        self.I(lambda cc=cc: e.scalar_tensor_tensor(
                            self.xres[:, cc, :], self.Bring[cc % 8][:, :],
                            self.cin[:, cc % 4:cc % 4 + 1], self.xres[:, cc, :],
                            ALU.mult, ALU.add, accum_out=self.S1[:, cc:cc + 1]),
                            sV, ("corr", cc))
                    for cc in range(gidx * 4, gidx * 4 + 4):
                        self.W(e, sC, ("gnsq", cc))
                        self.I(lambda cc=cc: e.tensor_scalar(
                            self.xbfv[cc][:, :], self.xres[:, cc, :],
                            self.vecs16[:, 16 + cc:17 + cc], None, ALU.mult),
                            sV, ("og", cc))
            if self.plan:
                for c in range(NDA):
                    g = c // 4
                    self.sV.marks[("corr", c)] = self.sV.marks[("corr", g * 4 + 3)]
                self.sV.marks[("corr", ND - 1)] = self.sV.marks[("corr", NDA - 1)]
                self.sC.marks = self.sC.marks

        if self.cut(w, e, 2, self.sV, ("corr", NDA - 1), self.xres[:, :, :]):
            return

        self.phase = "p3"
        if w == "v":
            if self.plan:
                self.sV.marks[("norm", ND - 1)] = self.sV.marks[("og", ND - 1)]
            self.W(e, sC, ("gnsq", ND - 1))
            self.I(lambda: e.tensor_reduce(self.pk2[:, 0:1],
                                           self.S1[:, :].rearrange("p (a c) -> p a c", a=1), AX, ALU.add), sV, "gnp1")
            self.W(e, sV, "gnp1")
            self.I(lambda: e.tensor_reduce(self.pk2[:, 1:2],
                                           self.S2[:, :].rearrange("p (a c) -> p a c", a=1), AX, ALU.add),
                   sV, "gnpk")
        if w == "v":
            self.W(e, ds["gn"], "gnbk")
            self.I(lambda: e.tensor_reduce(self.tot2[0:1, 0:2],
                                           self.stg[0:1, :].rearrange("p (r q) -> p q r", q=2),
                                           AX, ALU.add), sV, "gn1")
            self.W(e, sV, "gn1")
            self.I(lambda: e.tensor_scalar_mul(self.tot2[0:1, 2:3], self.tot2[0:1, 0:1],
                                               -1.0 / TD), sV, "gn2")
            self.I(lambda: e.tensor_scalar_mul(self.tot2[0:1, 3:4], self.tot2[0:1, 1:2],
                                               1.0 / TD), sV, "gn3")
            self.W(e, sV, "gn3")
            self.I(lambda: e.tensor_mul(self.tot2[0:1, 4:5], self.tot2[0:1, 2:3],
                                        self.tot2[0:1, 2:3]), sV, "gn4")
            self.W(e, sV, "gn4")
            self.I(lambda: e.tensor_sub(self.tot2[0:1, 5:6], self.tot2[0:1, 3:4],
                                        self.tot2[0:1, 4:5]), sV, "gn5")
            self.W(e, sV, "gn5")
            self.I(lambda: e.tensor_scalar_add(self.tot2[0:1, 5:6], self.tot2[0:1, 5:6],
                                               EPS), sV, "gnvar")
        if w == "v":
            self.W(e, sC, "gnrstd")
            self.I(lambda: e.tensor_copy(self.tot2[0:1, 3:4], self.tot2[0:1, 6:7]), sV, "gnpair")
        if w == "s":
            self._tap_after(e, sV, ("corr", 15), "p2", self.xres[:, :, :])
            self._tap_after(e, sV, ("norm", 15), "p3", self.abuf[:, :])

        if self.cut(w, e, 3, self.sV, ("norm", ND - 1), self.xres[:, :, :]):
            return

        self.phase = "p4"
        if w == "g":
            for c in range(5, ND):
                self._wo_load(e, c)
                if c == 8:
                    self.W(e, sC, "gnrow")
                    self.I(lambda: e.dma_start(out=self.st_in.ap()[:, :],
                                               in_=self.strow[:, :]), ds["gn"], "gnst", 16)
                    self.W(e, ds["gn"], "gnst")
                    self.I(lambda: e.collective_compute(
                        "AllGather", ALU.bypass, replica_groups=[list(range(NCORE))],
                        ins=[self.st_in.ap().opt()], outs=[self.st_out.ap().opt()]),
                        sCC, "gncc")
                    self.W(e, sCC, "gncc")
                    self.I(lambda: e.dma_start(
                        out=self.stg[:, :],
                        in_=self.st_out.ap().rearrange("(a r) q -> a (r q)", a=1)),
                        ds["gn"], "gnbk", 16)
            for c in range(ND):
                if c == 0:
                    self.W(e, sV, ("og", ND - 1))
                    self.W(e, sC, ("gnsq", ND - 1))
                elif c >= 4:
                    self.W(e, sT, ("pe_z1", c - 4))
                self.I(lambda c=c: e.dma_start(
                    out=self.trWoT[c % 4][:, :, :],
                    in_=self.wo_h[c * 128:(c + 1) * 128, 1536:2048].rearrange(
                        "p (k j) -> p k j", j=128)),
                    ds["ct" + str(c % 4)], ("ct", c), 16)
        if w == "t":
            self.W(e, sV, "gbw")
            self.W(e, sC, ("kev", NDA - 1))
            self.W(e, sC, ("vev", NDA - 1))
            self.W(e, sV, ("og", 11))
            for c in range(ND):
                par = c % 2
                ps = PA if par == 0 else PB
                trw = (lambda cc: self.trWoA[cc]) if c < 3 else (lambda cc: self.trWo[par])
                self.W(e, dps("co", c), ("co", c))
                if c >= 2:
                    self.W(e, sC, ("pAev", c - 2))
                if c == 6:
                    self.W(e, sV, "gnpk")
                    self.W(e, sV, "of")
                    self.I(lambda: e.matmul(PD[0:1, 1000:1002], self.onesf[:, 0:1],
                                            self.pk2[:, :], start=True, stop=True),
                           sT, "gnmm")
                for h in range(2):
                    for d in range(12):
                        last = (d == 11 and h == 1)
                        self.I(lambda ps=ps, c=c, d=d, h=h, trw=trw: e.matmul(
                            ps[:, h * 512:(h + 1) * 512],
                            trw(c)[:, d, :],
                            self.xbfv[d][:, h * 512:(h + 1) * 512],
                            start=(d == 0), stop=(d == 11)),
                            sT if last else None, ("pe_a", c) if last else None)
                for d in range(ND):
                    self.I(lambda c=c, d=d, trw=trw: e.matmul(
                        PD[:, 2 * c:2 * c + 2], trw(c)[:, d, :],
                        self.gbw[:, 2 * d:2 * d + 2],
                        start=(d == 0), stop=(d == ND - 1)),
                        sT if d == ND - 1 else None, ("wgw", c) if d == ND - 1 else None)
            self.W(e, sV, "gnpair")
            self.I(lambda: e.matmul(PC[:, 0:2], self.ones1r[:, :],
                                    self.tot2[0:1, 2:4],
                                    start=True, stop=True), sT, "gnbc")
            self.W(e, sV, ("og", ND - 1))
            for c in range(ND):
                par = c % 2
                ps = PA if par == 0 else PB
                self.W(e, ds["ct" + str(c % 4)], ("ct", c))
                if c < 2:
                    self.W(e, sC, ("pAev", ND - 2 + c))
                else:
                    self.W(e, sV, ("zadd", c - 2))
                for h in range(2):
                    for d in range(12, 16):
                        last = (d == 15 and h == 1)
                        self.I(lambda ps=ps, c=c, d=d, h=h: e.matmul(
                            ps[:, h * 512:(h + 1) * 512],
                            self.trWoT[c % 4][:, d - 12, :],
                            self.xbfv[d][:, h * 512:(h + 1) * 512],
                            start=(d == 12), stop=(d == 15)),
                            sT if last else None, ("pe_z1", c) if last else None)
            for c in range(ND):
                self.W(e, sV, ("z1bf", c))
                self.W(e, sC, ("z1sq", c))
                if c == 0:
                    self.W(e, sC, ("wgev", ND - 1))
                    self.W(e, sC, "gnmr")
                for h in range(2):
                    self.I(lambda c=c, h=h: e.matmul(
                        PC[0:1, h * 512:(h + 1) * 512], self.onesb[:, 0:1],
                        self.z1bfP[c % 2][:, h * 512:(h + 1) * 512],
                        start=(c == 0), stop=(c == ND - 1)))
                for h in range(2):
                    self.I(lambda c=c, h=h: e.matmul(
                        PD[0:1, h * 512:(h + 1) * 512], self.onesb[:, 0:1],
                        self.sqbfP[c % 2][:, h * 512:(h + 1) * 512],
                        start=(c == 0), stop=(c == ND - 1)), sT, ("st4", c))
        if w == "a":
            for c in range(ND):
                par = c % 2
                ps = PA if par == 0 else PB
                if c == 4:
                    for cc in range(ND - 4, ND):
                        self.W(e, sV, ("corr", cc))
                        self.I(lambda cc=cc: e.activation(
                            out=self.sqdump[:, :], in_=self.xres[:, cc, :],
                            func=AF.Square, accum_out=self.S2[:, cc:cc + 1]),
                            sC, ("gnsq", cc))
                self.W(e, sT, ("pe_a", c))
                self.I(lambda c=c, ps=ps: e.copy(out=self.z1[:, c, :], in_=ps[:, :]),
                       sC, ("pAev", c))
                if c == 6:
                    self.W(e, sT, "gnmm")
                    self.I(lambda: e.copy(out=self.strow[:, :], in_=PD[0:1, 1000:1002]),
                           sC, "gnrow")
                if c == 10:
                    self.W(e, sV, "gnvar")
                    self.I(lambda: e.activation(out=self.tot2[0:1, 5:6],
                                                in_=self.tot2[0:1, 5:6],
                                                func=AF.Ln), sC, "gnln")
                    self.W(e, sC, "gnln")
                    self.I(lambda: e.activation(out=self.tot2[0:1, 6:7],
                                                in_=self.tot2[0:1, 5:6],
                                                func=AF.Exp, scale=-0.5), sC, "gnrstd")
            self.W(e, sT, "gnbc")
            self.I(lambda: e.copy(out=self.mr_bc[:, :], in_=PC[:, 0:2]), sC, "gnmr")
            for c in range(ND):
                self.W(e, sT, ("wgw", c))
                self.I(lambda c=c: e.copy(out=self.wgwsb[:, 2 * c:2 * c + 2],
                                          in_=PD[:, 2 * c:2 * c + 2]), sC, ("wgev", c))
                if c >= 1:
                    self.W(e, sV, ("w2c", c - 1))
                    self.I(lambda c=c: e.activation(
                        out=self.z1[:, c - 1, :], in_=self.z1[:, c - 1, :],
                        func=AF.Identity, scale=self.mr_bc[:, 1:2],
                        bias=self.w2sb[:, c - 1:c]), sC, ("z1fx", c - 1))
            self.W(e, sV, ("w2c", ND - 1))
            self.I(lambda: e.activation(
                out=self.z1[:, ND - 1, :], in_=self.z1[:, ND - 1, :],
                func=AF.Identity, scale=self.mr_bc[:, 1:2],
                bias=self.w2sb[:, ND - 1:ND]), sC, ("z1fx", ND - 1))
            for c in range(ND):
                if c >= 2:
                    self.W(e, sT, ("st4", c - 2))
                self.I(lambda c=c: e.activation(out=self.sqbfP[c % 2][:, :],
                                                in_=self.z1[:, c, :],
                                                func=AF.Square), sC, ("z1sq", c))
        if w == "v":
            for c in range(ND):
                par = c % 2
                ps = PA if par == 0 else PB
                self.W(e, sT, ("pe_z1", c))
                self.I(lambda c=c, ps=ps: e.tensor_add(self.z1[:, c, :],
                                                       self.z1[:, c, :], ps[:, :]),
                       sV, ("zadd", c))
                if c == 0:
                    self.W(e, sC, "gnmr")
                    self.I(lambda: e.tensor_mul(self.s2c[:, :], self.mr_bc[:, 0:1],
                                                self.mr_bc[:, 1:2]), sV, "s2c")
                    self.W(e, sV, "s2c")
                self.W(e, sC, ("wgev", c))
                self.I(lambda c=c: e.scalar_tensor_tensor(
                    self.w2sb[:, c:c + 1],
                    self.wgwsb[:, 2 * c + 1:2 * c + 2], self.s2c[:, 0:1],
                    self.wgwsb[:, 2 * c:2 * c + 1],
                    ALU.mult, ALU.add), sV, ("w2c", c))
            for c in range(ND):
                self.W(e, sC, ("z1fx", c))
                if c >= 2:
                    self.W(e, sT, ("st4", c - 2))
                self.I(lambda c=c: e.tensor_copy(self.z1bfP[c % 2][:, :],
                                                 self.z1[:, c, :]), sV, ("z1bf", c))
        self._ln_rows(w, e, PC, PD, self.rows4, self.nmr_bc4, self.rstd_bc4,
                      pref="ln2", wait_tag=("st4", ND - 1))
        if w == "s":
            self._tap_after(e, sC, ("z1fx", ND - 1), "p4z", self.z1[:, :, :])
            self._tap_after(e, sC, ("wgev", ND - 1), "p4w", self.wgwsb[:, :])
            self._tap_after(e, sV, ("w2c", ND - 1), "p4b", self.w2sb[:, :])
            for c in range(ND):
                par = c % 2
                if c == 0:
                    self.W(e, ds["xsp"], "xsp")
                    self.W(e, sT, ("pe_r", NDA - 1))
                    if "p4z" in self.taps:
                        self.W(e, ds["tapq"], "tap_p4z")
                if c >= 2:
                    self.W(e, sV, ("h1", c - 2))
                self.I(lambda c=c, par=par: e.dma_start(
                    out=self.h0rl[par][:, :],
                    in_=self.h0_d.ap()[c * 128:(c + 1) * 128, :]), dps("h0", c), ("h0", c), 16)
        if w == "v":
            self.W(e, sC, "ln2bc")
            for c in range(ND):
                par = c % 2
                self.W(e, dps("h0", c), ("h0", c))
                self.I(lambda c=c, par=par: e.tensor_add(self.tmpa[:, :], self.h0rl[par][:, :],
                                                         self.nmr_bc4[:, :]))
                self.I(lambda c=c: e.tensor_mul(self.tmpb[:, :], self.z1[:, c, :],
                                                self.rstd_bc4[:, :]))
                self.I(lambda c=c: e.tensor_add(self.z1[:, c, :], self.tmpa[:, :],
                                                self.tmpb[:, :]), sV, ("h1", c))
                if c >= 2:
                    self.W(e, sT, ("st4b", c - 2))
                self.I(lambda c=c: e.tensor_copy(self.z1bfP[c % 2][:, :],
                                                 self.z1[:, c, :]), sV, ("h1bf", c))
        if w == "a":
            for c in range(ND):
                self.W(e, sV, ("h1", c))
                if c >= 2:
                    self.W(e, sT, ("st4b", c - 2))
                self.I(lambda c=c: e.activation(out=self.sqbfP[c % 2][:, :],
                                                in_=self.z1[:, c, :],
                                                func=AF.Square), sC, ("h1sq", c))
        if w == "t":
            for c in range(ND):
                self.W(e, sV, ("h1bf", c))
                self.W(e, sC, ("h1sq", c))
                if c == 0:
                    self.W(e, sC, "ln2bc")
                for h in range(2):
                    self.I(lambda c=c, h=h: e.matmul(
                        PC[0:1, h * 512:(h + 1) * 512], self.onesb[:, 0:1],
                        self.z1bfP[c % 2][:, h * 512:(h + 1) * 512],
                        start=(c == 0), stop=(c == ND - 1)))
                for h in range(2):
                    self.I(lambda c=c, h=h: e.matmul(
                        PD[0:1, h * 512:(h + 1) * 512], self.onesb[:, 0:1],
                        self.sqbfP[c % 2][:, h * 512:(h + 1) * 512],
                        start=(c == 0), stop=(c == ND - 1)), sT, ("st4b", c))
        self._ln_rows(w, e, PC, PD, self.rows4, self.nmr_bc4, self.rstd_bc4,
                      pref="ln3", wait_tag=("st4b", ND - 1))
        if w == "v":
            self.W(e, sC, "ln3bc")
            for c in range(ND):
                self.I(lambda c=c: e.tensor_mul(self.tmpa[:, :], self.z1[:, c, :],
                                                self.rstd_bc4[:, :]))
                self.I(lambda c=c: e.tensor_add(self.abuf[:, c * TC:(c + 1) * TC],
                                                self.tmpa[:, :], self.nmr_bc4[:, :]),
                       sV, ("fin", c))
        if w == "s":
            self.W(e, sV, ("h1", ND - 1))
            self.I(lambda: e.dma_start(out=self.h1_d.ap().rearrange("(c p) t -> p c t", p=128),
                                       in_=self.z1[:, :, :]), ds["h1sp"], "h1sp", 16)
            self._tap_after(e, sV, ("h1", ND - 1), "p4h", self.z1[:, :, :])
            self._tap_after(e, sV, ("fin", ND - 1), "p4f", self.abuf[:, :])

        if self.cut(w, e, 4, self.sV, ("fin", ND - 1), self.z1[:, :, :]):
            return

        self.phase = "p5"
        for half in (0, 1):
            ph = "p5a" if half == 0 else "p5c"
            self.phase = ph
            i0 = half * NIH
            if w == "g":
                for i in range(NIH):
                    par = i % 2
                    if i == 0 and half == 0:
                        self.W(e, sV, ("corr", ND - 1))
                        self.W(e, sC, ("gnsq", ND - 1))
                    if i == 0 and half == 1:
                        self.W(e, sT, ("p5apeu", NIH - 1))
                    if i >= 2:
                        self.W(e, sT, (ph + "peu", i - 2))
                    self.I(lambda i=i, par=par, i0=i0: e.dma_start(
                        out=self.trG[par][:, :, :],
                        in_=self.gw_h[(i0 + i) * 128:(i0 + i + 1) * 128, :].rearrange(
                            "p (k j) -> p k j", j=128)),
                        dps("cg", i), None, 16)
                    self.I(lambda i=i, par=par, i0=i0: e.dma_start(
                        out=self.trU[par][:, :, :],
                        in_=self.uw_h[(i0 + i) * 128:(i0 + i + 1) * 128, :].rearrange(
                            "p (k j) -> p k j", j=128)),
                        dps("cg", i), (ph + "cg", i), 16)
            if w == "t":
                for i in range(NIH):
                    par = i % 2
                    psg = PA if par == 0 else PC
                    psu = PB if par == 0 else PD
                    self.W(e, dps("cg", i), (ph + "cg", i))
                    if i == 1 and half == 0:
                        self.W(e, sC, "ln3bc")
                    if i == 0 and half == 1:
                        self.W(e, sC, ("p5b_z2ev", ND - 1))
                    if i >= 2:
                        self.W(e, sV, (ph + "hdn", i - 2))
                    for h in range(2):
                        for d in range(ND):
                            if i == 0 and half == 0 and h == 0:
                                self.W(e, sV, ("fin", d))
                            last = (d == ND - 1 and h == 1)
                            self.I(lambda psg=psg, i=i, d=d, h=h, par=par: e.matmul(
                                psg[:, h * 512:(h + 1) * 512], self.trG[par][:, d, :],
                                self.abuf[:, d * TC + h * 512: d * TC + (h + 1) * 512],
                                start=(d == 0), stop=(d == ND - 1)),
                                sT if last else None, (ph + "peg", i) if last else None)
                    for h in range(2):
                        for d in range(ND):
                            last = (d == ND - 1 and h == 1)
                            self.I(lambda psu=psu, i=i, d=d, h=h, par=par: e.matmul(
                                psu[:, h * 512:(h + 1) * 512], self.trU[par][:, d, :],
                                self.abuf[:, d * TC + h * 512: d * TC + (h + 1) * 512],
                                start=(d == 0), stop=(d == ND - 1)),
                                sT if last else None, (ph + "peu", i) if last else None)
            if w == "a":
                for i in range(NIH):
                    par = i % 2
                    psg = PA if par == 0 else PC
                    psu = PB if par == 0 else PD
                    self.W(e, sT, (ph + "peg", i))
                    if i >= 1:
                        self.W(e, sV, (ph + "hdn", i - 1))
                    self.I(lambda i=i, psg=psg, i0=i0: e.activation(
                        out=self.sgB[:, :], in_=psg[:, :], func=AF.Silu,
                        scale=self.vecs44[:, i0 + i:i0 + i + 1]), sC, (ph + "sg", i))
                    self.W(e, sT, (ph + "peu", i))
                    self.I(lambda i=i, psu=psu, i0=i0: e.activation(
                        out=self.uuB[:, :], in_=psu[:, :], func=AF.Copy,
                        scale=self.vecs44[:, 44 + i0 + i:44 + i0 + i + 1]), sC, (ph + "uu", i))
            if w == "v":
                for i in range(NIH):
                    self.W(e, sC, (ph + "uu", i))
                    if i == 0 and half == 0:
                        self.W(e, sV, ("fin", ND - 1))
                        self.W(e, ds["h1sp"], "h1sp")
                        self.W(e, sT, "ln3bcmm")
                    if i == 0 and half == 1:
                        self.W(e, sT, ("p5b_pez", ND - 1))
                    self.I(lambda i=i: e.tensor_mul(self.hdn1[:, i, :], self.sgB[:, :],
                                                    self.uuB[:, :]), sV, (ph + "hdn", i))
            if w == "s" and half == 0:
                self._tap_after(e, sV, ("p5ahdn", NIH - 1), "p5a", self.hdn1[:, :, :])

            phd = "p5b_" if half == 0 else "p6_"
            self.phase = phd
            if w == "g":
                for c in range(ND):
                    par = c % 2
                    if c == 0:
                        if half == 0:
                            self.W(e, sV, ("corr", ND - 1))
                        else:
                            self.W(e, sT, ("p5b_pez", ND - 1))
                    if c == 1 and half == 0:
                        self.W(e, sT, ("wgw", ND - 1))
                    if c >= 2:
                        self.W(e, sT, (phd + "pez", c - 2))
                    self.I(lambda c=c, par=par, half=half: e.dma_start(
                        out=self.trD[par][:, :, :],
                        in_=self.dw_h[c * 128:(c + 1) * 128,
                                      half * NIH * 128:(half + 1) * NIH * 128].rearrange(
                            "p (k j) -> p k j", j=128)),
                        dps("cd", c), (phd + "cd", c), 16)
            if w == "t":
                self.W(e, sV, (ph + "hdn", NIH - 1))
                for c in range(ND):
                    par = c % 2
                    ps = PA if par == 0 else PB
                    self.W(e, dps("cd", c), (phd + "cd", c))
                    if c == 0:
                        self.W(e, sC, (ph + "uu", NIH - 1))
                    if c >= 2:
                        if half == 0:
                            self.W(e, sC, (phd + "z2ev", c - 2))
                        else:
                            self.W(e, sV, (phd + "z2ac", c - 2))
                    for h in range(2):
                        for i in range(NIH):
                            last = (i == NIH - 1 and h == 1)
                            self.I(lambda ps=ps, c=c, i=i, h=h, par=par: e.matmul(
                                ps[:, h * 512:(h + 1) * 512], self.trD[par][:, i, :],
                                self.hdn1[:, i, h * 512:(h + 1) * 512],
                                start=(i == 0), stop=(i == NIH - 1)),
                                sT if last else None, (phd + "pez", c) if last else None)
                    if half == 1 and c >= 1:
                        self._st6(e, c - 1)
                if half == 1:
                    self._st6(e, ND - 1)
            if half == 0:
                if w == "a":
                    for c in range(ND):
                        par = c % 2
                        ps = PA if par == 0 else PB
                        self.W(e, sT, (phd + "pez", c))
                        if c == 0:
                            self.W(e, ds["h1sp"], "h1sp")
                        self.I(lambda c=c, ps=ps: e.activation(
                            out=self.z1[:, c, :], in_=ps[:, :], func=AF.Copy,
                            scale=self.vecs16[:, 48 + c:49 + c]), sC, (phd + "z2ev", c))
            else:
                if w == "v":
                    for c in range(ND):
                        par = c % 2
                        ps = PA if par == 0 else PB
                        self.W(e, sT, (phd + "pez", c))
                        self.I(lambda c=c, ps=ps: e.scalar_tensor_tensor(
                            self.z1[:, c, :], ps[:, :], self.vecs16[:, 48 + c:49 + c],
                            self.z1[:, c, :], ALU.mult, ALU.add), sV, (phd + "z2ac", c))
                        if c >= 1:
                            self.W(e, sT, ("st6", c - 1))
                        self.I(lambda c=c: e.tensor_copy(self.z2bf[:, :], self.z1[:, c, :]),
                               sV, ("z2bf", c))
                if w == "a":
                    for c in range(ND):
                        self.W(e, sV, (phd + "z2ac", c))
                        if c >= 1:
                            self.W(e, sT, ("st6", c - 1))
                        self.I(lambda c=c: e.activation(
                            out=self.sqbf6[:, :], in_=self.z1[:, c, :], func=AF.Square),
                            sC, ("z2sq", c))
        if w == "s":
            self._tap_after(e, sC, ("p5b_z2ev", ND - 1), "p5b", self.z1[:, :, :])
            self._tap_after(e, sV, ("p6_z2ac", ND - 1), "p6", self.z1[:, :, :])

        if self.cut(w, e, 6, self.sT, ("st6", ND - 1), self.z1[:, :, :]):
            return

        self.phase = "p7"
        self._ln_rows(w, e, PC, PD, self.rows6, self.nmr_bc7, self.rstd_bc7,
                      pref="ln4", wait_tag=("st6", ND - 1))
        if w == "s":
            def h1load(c):
                if c == 0:
                    self.W(e, ds["h1sp"], "h1sp")
                    self.W(e, sT, ("p5cpeu", NIH - 1))
                    self.W(e, sV, ("p5chdn", NIH - 1))
                self.I(lambda c=c: e.dma_start(
                    out=self.h1rl[c % 4][:, :],
                    in_=self.h1_d.ap()[c * 128:(c + 1) * 128, :]),
                    dps("h1r", c), ("h1r", c), 16)
            for c in range(4):
                h1load(c)
            for c in range(ND):
                if c % 2 == 0:
                    self.W(e, sV, ("out", c))
                    self.I(lambda c=c: e.dma_start(
                        out=self.y_h[c * 128:(c + 1) * 128, :],
                        in_=self.outb[c % 4][:, :]), dps("yq", c), ("yst", c), 16)
                if c + 4 < ND:
                    self.W(e, sV, ("hm", c))
                    h1load(c + 4)
            self.W(e, dps("yq", 0), ("yst", ND - 2))
            for _tn in self.taps:
                self.W(e, ds["tapq"], "tap_" + _tn)
        if w == "g":
            for c in range(1, ND, 2):
                self.W(e, sV, ("out", c))
                self.I(lambda c=c: e.dma_start(
                    out=self.y_h[c * 128:(c + 1) * 128, :],
                    in_=self.outb[c % 4][:, :]), dps("yq", c), ("yst", c), 16)
            self.W(e, dps("yq", 1), ("yst", ND - 1))
        if w == "v":
            self.W(e, sC, "ln4bc")
            for c in range(ND):
                self.W(e, dps("h1r", c), ("h1r", c))
                if c >= 4:
                    self.W(e, dps("yq", c - 4), ("yst", c - 4))
                self.I(lambda c=c: e.tensor_mul(self.tmp7a[:, :], self.z1[:, c, :],
                                                self.rstd_bc7[:, :]))
                self.I(lambda c=c: e.tensor_add(self.tmp7b[:, :], self.h1rl[c % 4][:, :],
                                                self.nmr_bc7[:, :]), sV, ("hm", c))
                self.I(lambda c=c: e.tensor_add(self.outb[c % 4][:, :], self.tmp7a[:, :],
                                                self.tmp7b[:, :]), sV, ("out", c))

    def _st6(self, e, c):
        PC, PD = self.PC, self.PD
        self.W(e, self.sV, ("z2bf", c))
        self.W(e, self.sC, ("z2sq", c))
        if c == 0:
            self.W(e, self.sC, ("p5cuu", NIH - 1))
        for h in range(2):
            self.I(lambda c=c, h=h: e.matmul(
                PC[0:1, h * 512:(h + 1) * 512], self.onesb[:, 0:1],
                self.z2bf[:, h * 512:(h + 1) * 512],
                start=(c == 0), stop=(c == ND - 1)))
        for h in range(2):
            self.I(lambda c=c, h=h: e.matmul(
                PD[0:1, h * 512:(h + 1) * 512], self.onesb[:, 0:1],
                self.sqbf6[:, h * 512:(h + 1) * 512],
                start=(c == 0), stop=(c == ND - 1)), self.sT, ("st6", c))

    def _ln_rows(self, w, e, PSm, PSq, rows, nmr_bc, rstd_bc, pref, wait_tag):
        sT, sV, sC = self.sT, self.sV, self.sC
        denom = float(D)
        if w == "a":
            self.W(e, sT, wait_tag)
            self.I(lambda: e.activation(out=rows[0:1, 0:TC], in_=PSm[0:1, 0:TC],
                                        func=AF.Copy, scale=-1.0 / denom))
            self.I(lambda: e.activation(out=rows[0:1, TC:2 * TC], in_=PSq[0:1, 0:TC],
                                        func=AF.Copy, scale=1.0 / denom), sC, pref + "rows")
        if w == "v":
            self.W(e, sC, pref + "rows")
            self.I(lambda: e.tensor_mul(rows[0:1, 2 * TC:3 * TC], rows[0:1, 0:TC],
                                        rows[0:1, 0:TC]))
            self.I(lambda: e.tensor_sub(rows[0:1, TC:2 * TC], rows[0:1, TC:2 * TC],
                                        rows[0:1, 2 * TC:3 * TC]))
            self.I(lambda: e.tensor_scalar_add(rows[0:1, TC:2 * TC], rows[0:1, TC:2 * TC],
                                               EPS), sV, pref + "var")
        if w == "a":
            self.W(e, sV, pref + "var")
            self.I(lambda: e.activation(out=rows[0:1, TC:2 * TC], in_=rows[0:1, TC:2 * TC],
                                        func=AF.Ln))
            self.I(lambda: e.activation(out=rows[0:1, TC:2 * TC], in_=rows[0:1, TC:2 * TC],
                                        func=AF.Exp, scale=-0.5), sC, pref + "rstd")
        if w == "v":
            self.W(e, sC, pref + "rstd")
            self.I(lambda: e.tensor_mul(rows[0:1, 0:TC], rows[0:1, 0:TC],
                                        rows[0:1, TC:2 * TC]), sV, pref + "nmr")
        if w == "t":
            self.W(e, sV, pref + "nmr")
            self.W(e, sV, "o1r")
            for h in range(2):
                self.I(lambda h=h: e.matmul(PSm[:, h * 512:(h + 1) * 512], self.ones1r[:, :],
                                            rows[0:1, h * 512:(h + 1) * 512],
                                            start=True, stop=True))
            for h in range(2):
                self.I(lambda h=h: e.matmul(PSq[:, h * 512:(h + 1) * 512], self.ones1r[:, :],
                                            rows[0:1, TC + h * 512:TC + (h + 1) * 512],
                                            start=True, stop=True), sT, pref + "bcmm")
        if w == "a":
            self.W(e, self.sT, pref + "bcmm")
            self.I(lambda: e.copy(out=nmr_bc[:, :], in_=PSm[:, :]))
            self.I(lambda: e.copy(out=rstd_bc[:, :], in_=PSq[:, :]), sC, pref + "bc")

    def _wo_load(self, e, c):
        dps = self.ps
        if c < 3:
            self.W(e, self.sV, ("scan", NDA - 1))
            self.I(lambda c=c: e.dma_start(
                out=self.trWoA[c][:, :, :],
                in_=self.wo_h[c * 128:(c + 1) * 128, :].rearrange(
                    "p (k j) -> p k j", j=128)),
                dps("co", c), ("co", c), 16)
            return
        if c == 3:
            self.W(e, self.sV, ("corr", 11))
        if c >= 5:
            self.W(e, self.sT, ("wgw", c - 2))
        self.I(lambda c=c: e.dma_start(
            out=self.trWo[c % 2][:, :, :],
            in_=self.wo_h[c * 128:(c + 1) * 128, :].rearrange(
                "p (k j) -> p k j", j=128)),
            dps("co", c), ("co", c), 16)

    def _carry_stage(self, e, g):
        self.W(e, self.sV, ("gt", g * 4 + 3))
        self.I(lambda g=g: e.dma_start(out=self.cc_in[g].ap()[:, :], in_=self.gt[:, :]),
               self.dsem["gts"], ("gts", g), 16)
        self.W(e, self.dsem["gts"], ("gts", g))
        if not NO_CC:
            self.I(lambda g=g: e.collective_compute(
                "AllGather", ALU.bypass, replica_groups=[list(range(NCORE))],
                ins=[self.cc_in[g].ap().opt()], outs=[self.cc_out[g].ap().opt()]),
                self.sCC, ("cc", g))

    def _carry_back(self, e, g):
        if NO_CC:
            if g >= 1:
                self.W(e, self.sV, ("cin", g - 1))
            self.I(lambda g=g: e.memset(self.gath[:, :], 0.0),
                   self.dsem["gtb"], ("gtb", g), 16)
            return
        self.W(e, self.sCC, ("cc", g))
        if g >= 1:
            self.W(e, self.sV, ("cin", g - 1))
        self.I(lambda g=g: e.dma_start(
            out=self.gath[:, :].rearrange("p (r j) -> p r j", j=4),
            in_=self.cc_out[g].ap().rearrange("(r p) j -> p r j", p=128)),
            self.dsem["gtb"], ("gtb", g), 16)

    def _tap(self, e, name, view):
        if name not in self.taps:
            return
        self.I(lambda: e.dma_start(out=self.taps[name], in_=view), self.dsem["tapq"], "tap_" + name, 16)

    def _tap_after(self, e, sem, tag, name, view):
        if name not in self.taps:
            return
        self.W(e, sem, tag)
        v = view.rearrange("p a b -> p (a b)") if len(view.shape) == 3 else view
        self.I(lambda: e.dma_start(out=self.taps[name], in_=v), self.dsem["tapq"], "tap_" + name, 16)


def _prep_inputs(L0_attn_x, L0_attn_Wr, L0_attn_Wk, L0_attn_Wv, L0_attn_Wo,
                 L0_attn_decay, L0_attn_gn_weight, L0_attn_gn_bias,
                 L0_ffn_gate_w, L0_ffn_up_w, L0_ffn_down_w,
                 L0_ffn_gate_s, L0_ffn_up_s, L0_ffn_down_s):
    xT = np.ascontiguousarray(np.asarray(L0_attn_x, np.float32)[0].T)

    def tile_lhsT(v):
        w = np.asarray(v, np.float32)
        J2, K2 = w.shape
        return np.ascontiguousarray(
            w.reshape(J2 // 128, 128, K2 // 128, 128).transpose(0, 3, 2, 1)
            .reshape(J2, K2))

    wr = tile_lhsT(L0_attn_Wr)
    wk = tile_lhsT(L0_attn_Wk)
    wv = tile_lhsT(L0_attn_Wv)
    wo = tile_lhsT(L0_attn_Wo)
    gw = tile_lhsT(L0_ffn_gate_w)
    uw = tile_lhsT(L0_ffn_up_w)
    dw = tile_lhsT(L0_ffn_down_w)

    def cols16(v):
        return np.ascontiguousarray(np.asarray(v, np.float32).reshape(16, 128).T)

    vecs16 = np.ascontiguousarray(np.concatenate(
        [cols16(L0_attn_decay), cols16(L0_attn_gn_weight),
         cols16(L0_attn_gn_bias), cols16(L0_ffn_down_s)], axis=1))

    def cols44(v):
        return np.asarray(v, np.float32).reshape(44, 128).T

    vecs44 = np.ascontiguousarray(
        np.concatenate([cols44(L0_ffn_gate_s), cols44(L0_ffn_up_s)], axis=1))

    in_maps = []
    for core in range(NCORE):
        iota = np.broadcast_to(
            np.arange(core * TC, (core + 1) * TC, dtype=np.float32)[None, :],
            (128, TC)).copy()
        cmask = np.zeros((128, 8, 4), np.float32)
        cmask[:, :core, :] = 1.0
        in_maps.append({
            "x": np.ascontiguousarray(xT[:, core * TC:(core + 1) * TC]),
            "wr": wr, "wk": wk, "wv": wv, "wo": wo,
            "gw": gw, "uw": uw, "dw": dw,
            "vecs16": vecs16, "vecs44": vecs44,
            "iota": iota, "cmask": np.ascontiguousarray(cmask.reshape(128, 32)),
        })
    return in_maps


_CACHED = {}


def kernel(**inputs):
    in_maps = _prep_inputs(**inputs)
    key = ("nc", TAPS, LIMIT, NO_CC, NDA, DBG0)
    if key not in _CACHED:
        _CACHED.clear()
        _CACHED[key] = Builder().build()
    nc = _CACHED[key]
    res = run_bass_kernel_spmd(nc, in_maps, core_ids=list(range(NCORE)))
    out = np.empty((T, D), np.float32)
    for core in range(NCORE):
        out[core * TC:(core + 1) * TC, :] = res.results[core]["y"].T
    kernel.last_results = res
    return out.reshape(1, T, D)


# version 18
# speedup vs baseline: 1.0339x; 1.0339x over previous
import numpy as np
import concourse.bass as bass
import concourse.mybir as mybir
from concourse.bass_utils import run_bass_kernel_spmd

F32 = mybir.dt.float32
BF16 = mybir.dt.bfloat16
ALU = mybir.AluOpType
AF = mybir.ActivationFunctionType
AX = mybir.AxisListType.X

D = 2048
ND = 16
II = 5632
NI = 44
NIH = 22
T = 8192
TC = 1024
NCORE = 8
TD = float(T * D)
EPS = 1e-5

TAPS = ()
LIMIT = 99
NO_CC = False
NDA = 16
DBG0 = False

KB = 1024
X0 = 0
Y0 = 64 * KB
Z0 = 128 * KB
ARENA_BYTES = 168 * KB


class Sem:

    def __init__(self, name):
        self.name = name
        self.h = None
        self.val = 0
        self.marks = {}

    def reset(self):
        self.val = 0

    def inc(self, tag=None, n=1):
        self.val += n
        if tag is not None:
            self.marks[tag] = self.val
        return self.val

    def at(self, tag):
        return self.marks[tag]


class Builder:
    def __init__(self):
        nc = bass.Bass("TRN2", target_bir_lowering=False, debug=False)
        self.nc = nc
        self.x_h = nc.dram_tensor("x", [D, TC], F32, kind="ExternalInput").ap()
        self.wr_h = nc.dram_tensor("wr", [D, D], F32, kind="ExternalInput").ap()
        self.wk_h = nc.dram_tensor("wk", [D, D], F32, kind="ExternalInput").ap()
        self.wv_h = nc.dram_tensor("wv", [D, D], F32, kind="ExternalInput").ap()
        self.wo_h = nc.dram_tensor("wo", [D, D], F32, kind="ExternalInput").ap()
        self.gw_h = nc.dram_tensor("gw", [II, D], F32, kind="ExternalInput").ap()
        self.uw_h = nc.dram_tensor("uw", [II, D], F32, kind="ExternalInput").ap()
        self.dw_h = nc.dram_tensor("dw", [D, II], F32, kind="ExternalInput").ap()
        self.vecs16_h = nc.dram_tensor("vecs16", [128, 64], F32, kind="ExternalInput").ap()
        self.vecs44_h = nc.dram_tensor("vecs44", [128, 88], F32, kind="ExternalInput").ap()
        self.iota_h = nc.dram_tensor("iota", [128, TC], F32, kind="ExternalInput").ap()
        self.cmask_h = nc.dram_tensor("cmask", [128, 32], F32, kind="ExternalInput").ap()
        self.y_h = nc.dram_tensor("y", [D, TC], F32, kind="ExternalOutput").ap()

        self.h0_d = nc.dram_tensor("h0d", [D, TC], F32)
        self.h1_d = nc.dram_tensor("h1d", [D, TC], F32)
        self.cc_in = [nc.dram_tensor(f"ccin{g}", [128, 4], F32) for g in range(4)]
        self.cc_out = [nc.dram_tensor(f"ccout{g}", [128 * NCORE, 4], F32) for g in range(4)]
        self.st_in = nc.dram_tensor("stin", [1, 2], F32)
        self.st_out = nc.dram_tensor("stout", [NCORE, 2], F32)

        self.taps = {}
        for t in TAPS:
            shp, dt_ = {
                "p1": ([128, ND * TC], BF16),
                "p2": ([128, ND * TC], F32),
                "p3": ([128, ND * TC], BF16),
                "p4h": ([128, ND * TC], F32),
                "p4z": ([128, ND * TC], F32),
                "p4w": ([128, 32], F32),
                "p4b": ([128, 16], F32),
                "p4f": ([128, ND * TC], BF16),
                "p5a": ([128, NIH * TC], BF16),
                "p5b": ([128, ND * TC], F32),
                "p6": ([128, ND * TC], F32),
            }[t]
            self.taps[t] = nc.dram_tensor("tap_" + t, shp, dt_, kind="ExternalOutput").ap()
        if DBG0:
            self.att0 = nc.dram_tensor("att0", [128, 4 * TC], F32, kind="ExternalOutput").ap()

        self.sT = Sem("sT")
        self.sV = Sem("sV")
        self.sC = Sem("sC")
        self.sCC = Sem("sCC")
        names = ["cst", "gts", "gtb", "gn", "xsp", "h1sp", "tapq",
                 "xld0", "xld1", "xld2", "xld3",
                 "ct0", "ct1", "ct2", "ct3"]
        for pn in ("cw", "co", "cg", "cd", "h0", "h1r", "yq"):
            names += [pn + "0", pn + "1"]
        self.dsem = {n: Sem(n) for n in names}


    def ps(self, base, idx):
        return self.dsem[base + str(idx % 2)]

    def cut(self, w, e, n, sem, tag, view):
        if LIMIT > n:
            return False
        if w == "s":
            self.W(e, sem, tag)
            self.I(lambda: e.dma_start(
                out=self.y_h.rearrange("(c p) t -> p c t", p=128), in_=view),
                self.dsem["yq0"], "ycut", 16)
            self.W(e, self.dsem["yq0"], "ycut")
        return True
    def carve(self, off, p, shape_free_elems, dt):
        esz = 2 if dt == BF16 else 4
        nb = int(np.prod(shape_free_elems)) * esz
        assert off + nb <= ARENA_BYTES, (off, nb)
        ap = self.arena.ap()[0:p, off:off + nb].bitcast(dt)
        if len(shape_free_elems) == 2:
            a, b = shape_free_elems
            ap = ap.rearrange("p (a b) -> p a b", b=b)
        return ap

    def build(self):
        nc = self.nc
        import contextlib
        es = contextlib.ExitStack()
        self.arena = es.enter_context(nc.sbuf_tensor("arena", [128, ARENA_BYTES], mybir.dt.uint8))
        self.abuf = es.enter_context(nc.sbuf_tensor("abuf", [128, ND * TC], BF16))
        self.iota = es.enter_context(nc.sbuf_tensor("iota_s", [128, TC], F32))
        self.vecs16 = es.enter_context(nc.sbuf_tensor("v16", [128, 64], F32))
        self.vecs44 = es.enter_context(nc.sbuf_tensor("v44", [128, 88], F32))
        self.ldc = es.enter_context(nc.sbuf_tensor("ldc", [128, 16], F32))
        self.nldc = es.enter_context(nc.sbuf_tensor("nldc", [128, 16], F32))
        self.dnsb = es.enter_context(nc.sbuf_tensor("dnsb", [128, 16], BF16))
        self.gsb = es.enter_context(nc.sbuf_tensor("gsb", [128, 64], F32))
        self.cmask = es.enter_context(nc.sbuf_tensor("cmask_s", [128, 32], F32))
        self.S1 = es.enter_context(nc.sbuf_tensor("S1", [128, 16], F32))
        self.S2 = es.enter_context(nc.sbuf_tensor("S2", [128, 16], F32))
        self.gt = es.enter_context(nc.sbuf_tensor("gt", [128, 4], F32))
        self.cin = es.enter_context(nc.sbuf_tensor("cin", [128, 4], F32))
        self.gath = es.enter_context(nc.sbuf_tensor("gath", [128, 32], F32))
        self.mm_sc = es.enter_context(nc.sbuf_tensor("mmsc", [128, 32], F32))
        self.onesb = es.enter_context(nc.sbuf_tensor("onesb", [128, 1], BF16))
        self.onesf = es.enter_context(nc.sbuf_tensor("onesf", [128, 1], F32))
        self.ones1r = es.enter_context(nc.sbuf_tensor("ones1r", [1, 128], F32))
        self.strow = es.enter_context(nc.sbuf_tensor("strow", [1, 2], F32))
        self.stg = es.enter_context(nc.sbuf_tensor("stg", [1, 16], F32))
        self.tot2 = es.enter_context(nc.sbuf_tensor("tot2", [1, 8], F32))
        self.mr_bc = es.enter_context(nc.sbuf_tensor("mrbc", [128, 2], F32))
        self.Agn = es.enter_context(nc.sbuf_tensor("Agn", [128, 16], F32))
        self.Bgn = es.enter_context(nc.sbuf_tensor("Bgn", [128, 16], F32))
        self.pk2 = es.enter_context(nc.sbuf_tensor("pk2", [128, 2], F32))
        self.gbw = es.enter_context(nc.sbuf_tensor("gbw", [128, 32], BF16))
        self.wgwsb = es.enter_context(nc.sbuf_tensor("wgwsb", [128, 32], F32))
        self.w2sb = es.enter_context(nc.sbuf_tensor("w2sb", [128, 16], F32))
        self.s2c = es.enter_context(nc.sbuf_tensor("s2c", [128, 1], F32))

        self.PA = es.enter_context(nc.psum_tensor("PA", [128, 1024], F32))
        self.PB = es.enter_context(nc.psum_tensor("PB", [128, 1024], F32))
        self.PC = es.enter_context(nc.psum_tensor("PC", [128, 1024], F32))
        self.PD = es.enter_context(nc.psum_tensor("PD", [128, 1024], F32))

        for s in self._all_sems():
            s.h = es.enter_context(nc.semaphore(s.name))

        cv = self.carve
        self.xres = cv(X0, 128, (ND, TC), F32)
        self.hdn1 = cv(X0, 128, (NIH, TC), BF16)
        pass
        self.z2bf = cv(X0 + 44 * KB, 128, (TC,), BF16)
        self.sqbf6 = cv(X0 + 46 * KB, 128, (TC,), BF16)
        self.rows6 = cv(X0 + 48 * KB, 1, (3 * TC,), F32)
        self.nmr_bc7 = cv(X0 + 60 * KB, 128, (TC,), F32)
        self.rstd_bc7 = cv(X0 + 44 * KB, 128, (TC,), F32)
        self.tmp7a = cv(X0 + 48 * KB, 128, (TC,), F32)
        self.tmp7b = cv(X0 + 52 * KB, 128, (TC,), F32)
        self.sqbf1 = cv(Z0 + 0 * KB, 128, (TC,), BF16)
        self.rows1 = cv(Z0 + 2 * KB, 1, (3 * TC,), F32)
        self.nmr_bc1 = cv(Z0 + 14 * KB, 128, (TC,), F32)
        self.rstd_bc1 = cv(Z0 + 18 * KB, 128, (TC,), F32)
        self.tmp1 = cv(Z0 + 22 * KB, 128, (TC,), F32)
        off = Y0 + 24 * KB
        self.trW = {}
        for w in ("r", "k", "v"):
            self.trW[w] = [cv(off + i * 4 * KB, 128, (ND, 128), BF16) for i in range(2)]
            off += 8 * KB
        self.kbuf = cv(Y0 + 48 * KB, 128, (TC,), F32)
        self.vbuf = cv(Y0 + 52 * KB, 128, (TC,), F32)
        self.scbuf = cv(Y0 + 56 * KB, 128, (TC,), F32)
        self.s0 = cv(Y0 + 60 * KB, 128, (TC,), F32)
        self.z1 = cv(Y0, 128, (ND, TC), F32)
        self.xbfv = [cv(X0 + c * 4 * KB, 128, (TC,), BF16) for c in range(ND)]
        self.trWoA = [cv(Y0 + 12 * KB + i * 4 * KB, 128, (ND, 128), BF16)
                      for i in range(3)]
        self.s1b = cv(Z0 + 0 * KB, 128, (TC,), F32)
        self.Bring = [cv(Z0 + 4 * KB + i * 4 * KB, 128, (TC,), F32) for i in range(8)]
        self.sqdump = cv(Z0 + 36 * KB, 128, (TC,), BF16)
        self.trWo = [cv(Z0 + 8 * KB + i * 4 * KB, 128, (ND, 128), BF16) for i in range(2)]
        self.trWoT = [cv(X0 + off * KB, 128, (4, 128), BF16)
                      for off in (54, 55, 58, 59)]
        self.trG = [cv(Z0 + 16 * KB + i * 4 * KB, 128, (ND, 128), BF16) for i in range(2)]
        self.trU = [cv(Z0 + 24 * KB + i * 4 * KB, 128, (ND, 128), BF16) for i in range(2)]
        self.sgB = cv(Z0 + 32 * KB, 128, (TC,), F32)
        self.uuB = cv(Z0 + 36 * KB, 128, (TC,), F32)
        self.trD = [cv(Z0 + 0 * KB + i * 6 * KB, 128, (NIH, 128), BF16) for i in range(2)]
        self.h1rl = [cv(Z0 + 24 * KB + i * 4 * KB, 128, (TC,), F32) for i in range(4)]
        self.outb = [cv(Z0 + 0 * KB + i * 4 * KB, 128, (TC,), F32) for i in range(4)]

        ab = self.abuf.ap()
        self.z1bfP = [ab[:, 14336:15360], ab[:, 15360:16384]]
        self.sqbfP = [ab[:, 12288:13312], ab[:, 13312:14336]]
        self.h0rl = [ab[:, 8192:10240].bitcast(F32), ab[:, 10240:12288].bitcast(F32)]
        self.tmpb = ab[:, 4096:6144].bitcast(F32)
        self.tmpa = cv(Z0 + 36 * KB, 128, (TC,), F32)
        self.rows4 = cv(X0 + 52 * KB, 1, (3 * TC,), F32)
        self.nmr_bc4 = cv(X0 + 44 * KB, 128, (TC,), F32)
        self.rstd_bc4 = cv(X0 + 48 * KB, 128, (TC,), F32)

        self.plan = True
        for w in ("g", "s", "a", "t", "v"):
            for s in self._all_sems():
                s.reset()
            self.stream(None, w)
        self.plan = False
        with nc.Block() as block:
            @block.gpsimd
            def _(g):
                self._emit(g, "g")

            @block.sync
            def _(s):
                self._emit(s, "s")

            @block.scalar
            def _(a):
                self._emit(a, "a")

            @block.tensor
            def _(t):
                self._emit(t, "t")

            @block.vector
            def _(v):
                self._emit(v, "v")
        es.close()
        return nc

    def _all_sems(self):
        return [self.sT, self.sV, self.sC, self.sCC] + list(self.dsem.values())

    def _emit(self, eng, which):
        for s in self._all_sems():
            s.reset()
        self.stream(eng, which)

    def I(self, fn, sem=None, tag=None, n=1):
        if not self.plan:
            with self.nc.named_scope(getattr(self, "phase", "p")):
                inst = fn()
            if sem is not None:
                inst.then_inc(sem.h, n)
        if sem is not None:
            sem.inc(tag, n)

    def W(self, eng, sem, tag):
        if self.plan:
            return
        eng.wait_ge(sem.h, sem.at(tag))

    def stream(self, e, w):
        PA, PB, PC, PD = self.PA, self.PB, self.PC, self.PD
        sT, sV, sC, sCC = self.sT, self.sV, self.sC, self.sCC
        ds = self.dsem
        dps = self.ps

        self.phase = "p0"
        if w in ("s", "g", "a"):
            for q in {"s": (0,), "g": (1,), "a": (2, 3)}[w]:
                self.I(lambda q=q: e.dma_start(
                    out=self.xres[:, 4 * q:4 * q + 4, :],
                    in_=self.x_h.rearrange("(c p) t -> p c t", p=128)[:, 4 * q:4 * q + 4, :]),
                    ds["xld" + str(q)], "xld", 16)
        if w == "g":
            self.I(lambda: e.dma_start(out=self.iota[:, :], in_=self.iota_h[:, :]), ds["cst"], None, 16)
            self.I(lambda: e.dma_start(out=self.vecs16[:, :], in_=self.vecs16_h[:, :]), ds["cst"], None, 16)
            self.I(lambda: e.dma_start(out=self.vecs44[:, :], in_=self.vecs44_h[:, :]), ds["cst"], None, 16)
            self.I(lambda: e.dma_start(out=self.cmask[:, :], in_=self.cmask_h[:, :]), ds["cst"], "cst", 16)
        if w == "a":
            self.W(e, ds["cst"], "cst")
            self.I(lambda: e.activation(out=self.gsb[:, 0:16], in_=self.vecs16[:, 0:16],
                                        func=AF.Exp, scale=-1.0), sC, "dexp")
            self.W(e, sV, "drec")
            self.I(lambda: e.activation(out=self.ldc[:, :], in_=self.gsb[:, 16:32],
                                        func=AF.Ln), sC, "dln")
        if w == "v":
            self.I(lambda: e.memset(self.onesb[:, :], 1.0), sV, "ob")
            self.I(lambda: e.memset(self.onesf[:, :], 1.0), sV, "of")
            self.I(lambda: e.memset(self.ones1r[:, :], 1.0), sV, "o1r")
            self.W(e, sC, "dexp")
            self.I(lambda: e.tensor_scalar_add(self.gsb[:, 16:32], self.gsb[:, 0:16], 1.0),
                   sV, "dadd")
            self.W(e, sV, "dadd")
            self.I(lambda: e.reciprocal(self.gsb[:, 32:48], self.gsb[:, 16:32]), sV, "drcp")
            self.W(e, sV, "drcp")
            self.I(lambda: e.tensor_scalar_max(self.gsb[:, 16:32], self.gsb[:, 32:48], 1e-7),
                   sV, "drec")
            self.W(e, sC, "dln")
            self.I(lambda: e.tensor_scalar_mul(self.nldc[:, :], self.ldc[:, :], -1.0), sV, "dnld")
            self.W(e, ds["cst"], "cst")
            self.I(lambda: e.tensor_copy(self.dnsb[:, :], self.vecs16[:, 48:64]), sV, "dnsb")
            self.I(lambda: e.tensor_copy(
                self.gbw[:, :].rearrange("p (d two) -> p d two", two=2)[:, :, 0:1],
                self.vecs16[:, 32:48].rearrange("p (d two) -> p d two", two=1)), sV, "gbw1")
            self.I(lambda: e.tensor_copy(
                self.gbw[:, :].rearrange("p (d two) -> p d two", two=2)[:, :, 1:2],
                self.vecs16[:, 16:32].rearrange("p (d two) -> p d two", two=1)), sV, "gbw")

        self.phase = "p1"
        if w == "v":
            for c in range(ND):
                if c % 4 == 0:
                    self.W(e, ds["xld" + str(c // 4)], "xld")
                self.I(lambda c=c: e.tensor_copy(self.abuf[:, c * TC:(c + 1) * TC],
                                                 self.xres[:, c, :]), sV, ("xbf", c))
        if w == "a":
            for c in range(ND):
                if c % 4 == 0:
                    self.W(e, ds["xld" + str(c // 4)], "xld")
                if c >= 1:
                    self.W(e, sT, ("st1", c - 1))
                self.I(lambda c=c: e.activation(out=self.sqbf1[:, :], in_=self.xres[:, c, :],
                                                func=AF.Square), sC, ("xsq", c))
        if w == "t":
            self.W(e, sV, "ob")
            for c in range(ND):
                self.W(e, sV, ("xbf", c))
                self.W(e, sC, ("xsq", c))
                for h in range(2):
                    self.I(lambda c=c, h=h: e.matmul(
                        PC[0:1, h * 512:(h + 1) * 512], self.onesb[:, 0:1],
                        self.abuf[:, c * TC + h * 512: c * TC + (h + 1) * 512],
                        start=(c == 0), stop=(c == ND - 1)))
                for h in range(2):
                    self.I(lambda c=c, h=h: e.matmul(
                        PD[0:1, h * 512:(h + 1) * 512], self.onesb[:, 0:1],
                        self.sqbf1[:, h * 512:(h + 1) * 512],
                        start=(c == 0), stop=(c == ND - 1)), sT, ("st1", c))
        self._ln_rows(w, e, PC, PD, self.rows1, self.nmr_bc1, self.rstd_bc1,
                      pref="ln1", wait_tag=("st1", ND - 1))
        if w == "v":
            self.W(e, sC, "ln1bc")
            for c in range(ND):
                self.I(lambda c=c: e.tensor_mul(self.tmp1[:, :], self.xres[:, c, :],
                                                self.rstd_bc1[:, :]))
                self.I(lambda c=c: e.tensor_add(self.abuf[:, c * TC:(c + 1) * TC],
                                                self.tmp1[:, :], self.nmr_bc1[:, :]),
                       sV, ("ain", c))
        
        if self.cut(w, e, 1, self.sV, ("ain", ND - 1), self.xres[:, :, :]):
            return

        self.phase = "p2"
        G = NDA // 4
        if w == "g":
            for c in range(3):
                self._wo_load(e, c)
            for c in range(NDA):
                par = c % 2
                if c >= 2:
                    self.W(e, sT, ("pe_r", c - 2))
                for wn, hsrc in (("r", self.wr_h), ("k", self.wk_h), ("v", self.wv_h)):
                    self.I(lambda wn=wn, hsrc=hsrc, c=c, par=par: e.dma_start(
                        out=self.trW[wn][par][:, :, :],
                        in_=hsrc[c * 128:(c + 1) * 128, :].rearrange(
                            "p (k j) -> p k j", j=128)), dps("cw", c),
                        ("cw", c) if wn == "v" else None, 16)
                if DBG0 and c == 0:
                    self.W(e, sC, ("vev", 0))
                    self.I(lambda: e.dma_start(out=self.att0[:, 0:TC], in_=self.kbuf[:, :]),
                           ds["tapq"], "d0a", 16)
                    self.I(lambda: e.dma_start(out=self.att0[:, TC:2 * TC], in_=self.vbuf[:, :]),
                           ds["tapq"], "d0b", 16)
                    self.W(e, sV, ("A", 0))
                    self.I(lambda: e.dma_start(out=self.att0[:, 2 * TC:3 * TC], in_=self.s1b[:, :]),
                           ds["tapq"], "d0c", 16)
                    self.I(lambda: e.dma_start(out=self.att0[:, 3 * TC:4 * TC], in_=self.Bring[0][:, :]),
                           ds["tapq"], "d0d", 16)
                for g in range(G):
                    if 4 * g + 5 == c:
                        self._carry_stage(e, g)
                    if 4 * g + 7 == c:
                        self._carry_back(e, g)
            for g in range(G):
                if 4 * g + 5 >= NDA:
                    self._carry_stage(e, g)
            for c in range(3, 9):
                self._wo_load(e, c)
            for g in range(G):
                if 4 * g + 7 > NDA - 1:
                    self._carry_back(e, g)
        if w == "s":
            self.W(e, sV, ("ain", ND - 1))
            self.I(lambda: e.dma_start(
                out=self.h0_d.ap().rearrange("(c p) t -> p c t", p=128),
                in_=self.xres[:, :, :]), ds["xsp"], "xsp", 16)
            self._tap(e, "p1", self.abuf[:, :])
        if w == "t":
            for c in range(NDA):
                par = c % 2
                self.W(e, dps("cw", c), ("cw", c))
                for wn, ps in (("k", PA), ("v", PB), ("r", PC)):
                    if c >= 1:
                        self.W(e, sC, ({"k": "kev", "v": "vev", "r": "rexp"}[wn], c - 1))
                    for h in range(2):
                        for d in range(ND):
                            if c == 0 and wn == "k" and h == 0:
                                self.W(e, sV, ("ain", d))
                            last = (d == ND - 1 and h == 1)
                            self.I(lambda wn=wn, ps=ps, c=c, d=d, h=h, par=par: e.matmul(
                                ps[:, h * 512:(h + 1) * 512],
                                self.trW[wn][par][:, d, :],
                                self.abuf[:, d * TC + h * 512: d * TC + (h + 1) * 512],
                                start=(d == 0), stop=(d == ND - 1)),
                                sT if last else None, ("pe_" + wn, c) if last else None)
        if w == "a":
            self.W(e, ds["cst"], "cst")
            self.W(e, sV, "dnld")
            for c in range(NDA):
                par = c % 2
                self.W(e, sT, ("pe_k", c))
                if c >= 1:
                    self.W(e, sV, ("kvs", c - 1))
                self.I(lambda c=c: e.copy(out=self.kbuf[:, :], in_=PA[:, :]), sC, ("kev", c))
                self.W(e, sT, ("pe_v", c))
                self.I(lambda c=c: e.copy(out=self.vbuf[:, :], in_=PB[:, :]), sC, ("vev", c))
                self.W(e, sT, ("pe_r", c))
                if DBG0 and c == 1:
                    self.W(e, ds["tapq"], "d0d")
                if c >= 1:
                    self.W(e, sV, ("A", c - 1))
                self.I(lambda c=c: e.activation(out=self.s1b[:, :], in_=PC[:, :],
                                                func=AF.Tanh, scale=0.5), sC, ("rexp", c))
                self.I(lambda c=c: e.activation(out=self.scbuf[:, :], in_=self.iota[:, :],
                                                func=AF.Exp, scale=self.ldc[:, c:c + 1]),
                       sC, ("scl", c))
                self.W(e, sV, ("inva", c))
                self.I(lambda c=c: e.activation(out=self.s0[:, :], in_=self.s0[:, :],
                                                func=AF.Exp), sC, ("inv", c))
                glist = []
                if c % 4 == 3 and c >= 7 and c < NDA - 1:
                    glist.append(c // 4 - 1)
                if c == NDA - 1:
                    glist += [g for g in range(NDA // 4 - 1) if 4 * g + 7 >= NDA - 1]
                for gidx in glist:
                    for cc in range(gidx * 4, gidx * 4 + 4):
                        self.W(e, sV, ("corr", cc))
                        self.I(lambda cc=cc: e.activation(
                            out=self.sqdump[:, :], in_=self.xres[:, cc, :],
                            func=AF.Square, accum_out=self.S2[:, cc:cc + 1]),
                            sC, ("gnsq", cc))
        if w == "v":
            for c in range(NDA):
                bslot = c % 8
                if c == 0:
                    self.W(e, ds["cst"], "cst")
                if DBG0 and c == 1:
                    self.W(e, ds["tapq"], "d0d")
                self.W(e, sC, ("scl", c))
                self.I(lambda c=c, bslot=bslot: e.scalar_tensor_tensor(
                    self.Bring[bslot][:, :], self.s1b[:, :], 1.0, self.scbuf[:, :],
                    ALU.add, ALU.mult), sV, ("rr", c))
                self.I(lambda c=c: e.tensor_scalar(
                    self.s0[:, :], self.iota[:, :], self.nldc[:, c:c + 1], 23.0258509299,
                    ALU.mult, ALU.min), sV, ("inva", c))
                if DBG0 and c == 0:
                    self.W(e, ds["tapq"], "d0b")
                self.W(e, sC, ("inv", c))
                self.I(lambda c=c: e.tensor_mul(self.kbuf[:, :], self.kbuf[:, :],
                                                self.vbuf[:, :]), sV, ("kv", c))
                self.I(lambda c=c: e.scalar_tensor_tensor(
                    self.s0[:, :], self.s0[:, :], 1e10, self.kbuf[:, :],
                    ALU.min, ALU.mult), sV, ("kvs", c))
                self.I(lambda c=c: e.tensor_tensor_scan(
                    self.s1b[:, :], self.s0[:, :], self.s0[:, :], 0.0,
                    ALU.add, ALU.bypass), sV, ("scan", c))
                if self.plan:
                    self.sV.marks[("rs", c)] = self.sV.val
                if c == 0:
                    self.W(e, ds["xsp"], "xsp")
                self.I(lambda c=c, bslot=bslot: e.tensor_mul(
                    self.xres[:, c, :], self.s1b[:, :], self.Bring[bslot][:, :]),
                    sV, ("A", c))
                if c >= 4:
                    self.W(e, ds["gts"], ("gts", c // 4 - 1))
                self.I(lambda c=c: e.tensor_copy(self.gt[:, c % 4:c % 4 + 1],
                                                 self.s1b[:, 1023:1024]), sV, ("gt", c))
                glist = []
                if c % 4 == 3 and c >= 7 and c < NDA - 1:
                    glist.append(c // 4 - 1)
                if c == NDA - 1:
                    glist += [g for g in range(G) if 4 * g + 7 >= NDA - 1]
                for gidx in glist:
                    self.W(e, ds["gtb"], ("gtb", gidx))
                    if gidx == 0:
                        self.W(e, ds["cst"], "cst")
                    self.I(lambda: e.tensor_mul(self.mm_sc[:, :], self.gath[:, :],
                                                self.cmask[:, :]), sV, ("mmg", gidx))
                    self.W(e, sV, ("mmg", gidx))
                    self.I(lambda g=gidx: e.tensor_reduce(
                        self.cin[:, :],
                        self.mm_sc[:, :].rearrange("p (r j) -> p j r", j=4),
                        AX, ALU.add), sV, ("cin", gidx))
                    self.W(e, sV, ("cin", gidx))
                    for cc in range(gidx * 4, gidx * 4 + 4):
                        self.I(lambda cc=cc: e.scalar_tensor_tensor(
                            self.xres[:, cc, :], self.Bring[cc % 8][:, :],
                            self.cin[:, cc % 4:cc % 4 + 1], self.xres[:, cc, :],
                            ALU.mult, ALU.add, accum_out=self.S1[:, cc:cc + 1]),
                            sV, ("corr", cc))
                    for cc in range(gidx * 4, gidx * 4 + 4):
                        self.W(e, sC, ("gnsq", cc))
                        self.I(lambda cc=cc: e.tensor_scalar(
                            self.xbfv[cc][:, :], self.xres[:, cc, :],
                            self.vecs16[:, 16 + cc:17 + cc], None, ALU.mult),
                            sV, ("og", cc))
            if self.plan:
                for c in range(NDA):
                    g = c // 4
                    self.sV.marks[("corr", c)] = self.sV.marks[("corr", g * 4 + 3)]
                self.sV.marks[("corr", ND - 1)] = self.sV.marks[("corr", NDA - 1)]
                self.sC.marks = self.sC.marks

        if self.cut(w, e, 2, self.sV, ("corr", NDA - 1), self.xres[:, :, :]):
            return

        self.phase = "p3"
        if w == "v":
            if self.plan:
                self.sV.marks[("norm", ND - 1)] = self.sV.marks[("og", ND - 1)]
            self.W(e, sC, ("gnsq", ND - 1))
            self.I(lambda: e.tensor_reduce(self.pk2[:, 0:1],
                                           self.S1[:, :].rearrange("p (a c) -> p a c", a=1), AX, ALU.add), sV, "gnp1")
            self.W(e, sV, "gnp1")
            self.I(lambda: e.tensor_reduce(self.pk2[:, 1:2],
                                           self.S2[:, :].rearrange("p (a c) -> p a c", a=1), AX, ALU.add),
                   sV, "gnpk")
        if w == "s":
            self._tap_after(e, sV, ("corr", 15), "p2", self.xres[:, :, :])
            self._tap_after(e, sV, ("norm", 15), "p3", self.abuf[:, :])

        if self.cut(w, e, 3, self.sV, ("norm", ND - 1), self.xres[:, :, :]):
            return

        self.phase = "p4"
        if w == "g":
            for c in range(9, ND):
                self._wo_load(e, c)
            def wot_load(c):
                if c == 0:
                    self.W(e, sV, ("og", ND - 1))
                    self.W(e, sC, ("gnsq", ND - 1))
                elif c >= 4:
                    self.W(e, sT, ("pe_z1", c - 4))
                self.I(lambda c=c: e.dma_start(
                    out=self.trWoT[c % 4][:, :, :],
                    in_=self.wo_h[c * 128:(c + 1) * 128, 1536:2048].rearrange(
                        "p (k j) -> p k j", j=128)),
                    ds["ct" + str(c % 4)], ("ct", c), 16)
            for c in range(4):
                wot_load(c)
            self.W(e, sC, "gnrow")
            self.I(lambda: e.dma_start(out=self.st_in.ap()[:, :],
                                       in_=self.strow[:, :]), ds["gn"], "gnst", 16)
            self.W(e, ds["gn"], "gnst")
            self.I(lambda: e.collective_compute(
                "AllGather", ALU.bypass, replica_groups=[list(range(NCORE))],
                ins=[self.st_in.ap().opt()], outs=[self.st_out.ap().opt()]),
                sCC, "gncc")
            self.W(e, sCC, "gncc")
            self.I(lambda: e.dma_start(
                out=self.stg[:, :],
                in_=self.st_out.ap().rearrange("(a r) q -> a (r q)", a=1)),
                ds["gn"], "gnbk", 16)
            for c in range(4, ND):
                wot_load(c)
        if w == "t":
            self.W(e, sV, "gbw")
            self.W(e, sC, ("kev", NDA - 1))
            self.W(e, sC, ("vev", NDA - 1))
            self.W(e, sV, ("og", 11))
            for c in range(ND):
                par = c % 2
                ps = PA if par == 0 else PB
                trw = (lambda cc: self.trWoA[cc]) if c < 3 else (lambda cc: self.trWo[par])
                self.W(e, dps("co", c), ("co", c))
                if c >= 2:
                    self.W(e, sC, ("pAev", c - 2))
                if c == 12:
                    self.W(e, sV, "gnpk")
                    self.W(e, sV, "of")
                    self.I(lambda: e.matmul(PD[0:1, 1000:1002], self.onesf[:, 0:1],
                                            self.pk2[:, :], start=True, stop=True),
                           sT, "gnmm")
                for h in range(2):
                    for d in range(12):
                        last = (d == 11 and h == 1)
                        self.I(lambda ps=ps, c=c, d=d, h=h, trw=trw: e.matmul(
                            ps[:, h * 512:(h + 1) * 512],
                            trw(c)[:, d, :],
                            self.xbfv[d][:, h * 512:(h + 1) * 512],
                            start=(d == 0), stop=(d == 11)),
                            sT if last else None, ("pe_a", c) if last else None)
                for d in range(ND):
                    self.I(lambda c=c, d=d, trw=trw: e.matmul(
                        PD[:, 2 * c:2 * c + 2], trw(c)[:, d, :],
                        self.gbw[:, 2 * d:2 * d + 2],
                        start=(d == 0), stop=(d == ND - 1)),
                        sT if d == ND - 1 else None, ("wgw", c) if d == ND - 1 else None)
            self.W(e, sV, ("og", ND - 1))
            for c in range(ND):
                par = c % 2
                ps = PA if par == 0 else PB
                self.W(e, ds["ct" + str(c % 4)], ("ct", c))
                if c < 2:
                    self.W(e, sC, ("pAev", ND - 2 + c))
                else:
                    self.W(e, sV, ("zadd", c - 2))
                for h in range(2):
                    for d in range(12, 16):
                        last = (d == 15 and h == 1)
                        self.I(lambda ps=ps, c=c, d=d, h=h: e.matmul(
                            ps[:, h * 512:(h + 1) * 512],
                            self.trWoT[c % 4][:, d - 12, :],
                            self.xbfv[d][:, h * 512:(h + 1) * 512],
                            start=(d == 12), stop=(d == 15)),
                            sT if last else None, ("pe_z1", c) if last else None)
            self.W(e, sV, "gnpair")
            self.I(lambda: e.matmul(PC[:, 0:2], self.ones1r[:, :],
                                    self.tot2[0:1, 2:4],
                                    start=True, stop=True), sT, "gnbc")
            for c in range(ND):
                self.W(e, sV, ("z1bf", c))
                self.W(e, sC, ("z1sq", c))
                if c == 0:
                    self.W(e, sC, ("wgev", ND - 1))
                    self.W(e, sC, "gnmr")
                for h in range(2):
                    self.I(lambda c=c, h=h: e.matmul(
                        PC[0:1, h * 512:(h + 1) * 512], self.onesb[:, 0:1],
                        self.z1bfP[c % 2][:, h * 512:(h + 1) * 512],
                        start=(c == 0), stop=(c == ND - 1)))
                for h in range(2):
                    self.I(lambda c=c, h=h: e.matmul(
                        PD[0:1, h * 512:(h + 1) * 512], self.onesb[:, 0:1],
                        self.sqbfP[c % 2][:, h * 512:(h + 1) * 512],
                        start=(c == 0), stop=(c == ND - 1)), sT, ("st4", c))
        if w == "a":
            for c in range(ND):
                par = c % 2
                ps = PA if par == 0 else PB
                if c == 8:
                    for cc in range(ND - 4, ND):
                        self.W(e, sV, ("corr", cc))
                        self.I(lambda cc=cc: e.activation(
                            out=self.sqdump[:, :], in_=self.xres[:, cc, :],
                            func=AF.Square, accum_out=self.S2[:, cc:cc + 1]),
                            sC, ("gnsq", cc))
                self.W(e, sT, ("pe_a", c))
                self.I(lambda c=c, ps=ps: e.copy(out=self.z1[:, c, :], in_=ps[:, :]),
                       sC, ("pAev", c))
                if c == 12:
                    self.W(e, sT, "gnmm")
                    self.I(lambda: e.copy(out=self.strow[:, :], in_=PD[0:1, 1000:1002]),
                           sC, "gnrow")
            self.W(e, sV, "gnvar")
            self.I(lambda: e.activation(out=self.tot2[0:1, 5:6],
                                        in_=self.tot2[0:1, 5:6],
                                        func=AF.Ln), sC, "gnln")
            self.W(e, sC, "gnln")
            self.I(lambda: e.activation(out=self.tot2[0:1, 6:7],
                                        in_=self.tot2[0:1, 5:6],
                                        func=AF.Exp, scale=-0.5), sC, "gnrstd")
            self.W(e, sT, "gnbc")
            self.I(lambda: e.copy(out=self.mr_bc[:, :], in_=PC[:, 0:2]), sC, "gnmr")
            for c in range(ND):
                self.W(e, sT, ("wgw", c))
                self.I(lambda c=c: e.copy(out=self.wgwsb[:, 2 * c:2 * c + 2],
                                          in_=PD[:, 2 * c:2 * c + 2]), sC, ("wgev", c))
                if c >= 1:
                    self.W(e, sV, ("w2c", c - 1))
                    self.I(lambda c=c: e.activation(
                        out=self.z1[:, c - 1, :], in_=self.z1[:, c - 1, :],
                        func=AF.Identity, scale=self.mr_bc[:, 1:2],
                        bias=self.w2sb[:, c - 1:c]), sC, ("z1fx", c - 1))
            self.W(e, sV, ("w2c", ND - 1))
            self.I(lambda: e.activation(
                out=self.z1[:, ND - 1, :], in_=self.z1[:, ND - 1, :],
                func=AF.Identity, scale=self.mr_bc[:, 1:2],
                bias=self.w2sb[:, ND - 1:ND]), sC, ("z1fx", ND - 1))
            for c in range(ND):
                if c >= 2:
                    self.W(e, sT, ("st4", c - 2))
                self.I(lambda c=c: e.activation(out=self.sqbfP[c % 2][:, :],
                                                in_=self.z1[:, c, :],
                                                func=AF.Square), sC, ("z1sq", c))
        if w == "v":
            for c in range(ND):
                par = c % 2
                ps = PA if par == 0 else PB
                self.W(e, sT, ("pe_z1", c))
                self.I(lambda c=c, ps=ps: e.tensor_add(self.z1[:, c, :],
                                                       self.z1[:, c, :], ps[:, :]),
                       sV, ("zadd", c))
                if c == 2:
                    self.W(e, ds["gn"], "gnbk")
                    self.I(lambda: e.tensor_reduce(
                        self.tot2[0:1, 0:2],
                        self.stg[0:1, :].rearrange("p (r q) -> p q r", q=2),
                        AX, ALU.add), sV, "gn1")
                    self.W(e, sV, "gn1")
                    self.I(lambda: e.tensor_scalar_mul(self.tot2[0:1, 2:3],
                                                       self.tot2[0:1, 0:1],
                                                       -1.0 / TD), sV, "gn2")
                    self.I(lambda: e.tensor_scalar_mul(self.tot2[0:1, 3:4],
                                                       self.tot2[0:1, 1:2],
                                                       1.0 / TD), sV, "gn3")
                    self.W(e, sV, "gn3")
                    self.I(lambda: e.tensor_mul(self.tot2[0:1, 4:5], self.tot2[0:1, 2:3],
                                                self.tot2[0:1, 2:3]), sV, "gn4")
                    self.W(e, sV, "gn4")
                    self.I(lambda: e.tensor_sub(self.tot2[0:1, 5:6], self.tot2[0:1, 3:4],
                                                self.tot2[0:1, 4:5]), sV, "gn5")
                    self.W(e, sV, "gn5")
                    self.I(lambda: e.tensor_scalar_add(self.tot2[0:1, 5:6],
                                                       self.tot2[0:1, 5:6],
                                                       EPS), sV, "gnvar")
                    self.W(e, sC, "gnrstd")
                    self.I(lambda: e.tensor_copy(self.tot2[0:1, 3:4],
                                                 self.tot2[0:1, 6:7]), sV, "gnpair")
            self.W(e, sC, "gnmr")
            self.I(lambda: e.tensor_mul(self.s2c[:, :], self.mr_bc[:, 0:1],
                                        self.mr_bc[:, 1:2]), sV, "s2c")
            self.W(e, sV, "s2c")
            for c in range(ND):
                self.W(e, sC, ("wgev", c))
                self.I(lambda c=c: e.scalar_tensor_tensor(
                    self.w2sb[:, c:c + 1],
                    self.wgwsb[:, 2 * c + 1:2 * c + 2], self.s2c[:, 0:1],
                    self.wgwsb[:, 2 * c:2 * c + 1],
                    ALU.mult, ALU.add), sV, ("w2c", c))
            for c in range(ND):
                self.W(e, sC, ("z1fx", c))
                if c >= 2:
                    self.W(e, sT, ("st4", c - 2))
                self.I(lambda c=c: e.tensor_copy(self.z1bfP[c % 2][:, :],
                                                 self.z1[:, c, :]), sV, ("z1bf", c))
        self._ln_rows(w, e, PC, PD, self.rows4, self.nmr_bc4, self.rstd_bc4,
                      pref="ln2", wait_tag=("st4", ND - 1))
        if w == "s":
            self._tap_after(e, sC, ("z1fx", ND - 1), "p4z", self.z1[:, :, :])
            self._tap_after(e, sC, ("wgev", ND - 1), "p4w", self.wgwsb[:, :])
            self._tap_after(e, sV, ("w2c", ND - 1), "p4b", self.w2sb[:, :])
            for c in range(ND):
                par = c % 2
                if c == 0:
                    self.W(e, ds["xsp"], "xsp")
                    self.W(e, sT, ("pe_r", NDA - 1))
                    if "p4z" in self.taps:
                        self.W(e, ds["tapq"], "tap_p4z")
                if c >= 2:
                    self.W(e, sV, ("h1", c - 2))
                self.I(lambda c=c, par=par: e.dma_start(
                    out=self.h0rl[par][:, :],
                    in_=self.h0_d.ap()[c * 128:(c + 1) * 128, :]), dps("h0", c), ("h0", c), 16)
        if w == "v":
            self.W(e, sC, "ln2bc")
            for c in range(ND):
                par = c % 2
                self.W(e, dps("h0", c), ("h0", c))
                self.I(lambda c=c, par=par: e.tensor_add(self.tmpa[:, :], self.h0rl[par][:, :],
                                                         self.nmr_bc4[:, :]))
                self.I(lambda c=c: e.tensor_mul(self.tmpb[:, :], self.z1[:, c, :],
                                                self.rstd_bc4[:, :]))
                self.I(lambda c=c: e.tensor_add(self.z1[:, c, :], self.tmpa[:, :],
                                                self.tmpb[:, :]), sV, ("h1", c))
                if c >= 2:
                    self.W(e, sT, ("st4b", c - 2))
                self.I(lambda c=c: e.tensor_copy(self.z1bfP[c % 2][:, :],
                                                 self.z1[:, c, :]), sV, ("h1bf", c))
        if w == "a":
            for c in range(ND):
                self.W(e, sV, ("h1", c))
                if c >= 2:
                    self.W(e, sT, ("st4b", c - 2))
                self.I(lambda c=c: e.activation(out=self.sqbfP[c % 2][:, :],
                                                in_=self.z1[:, c, :],
                                                func=AF.Square), sC, ("h1sq", c))
        if w == "t":
            for c in range(ND):
                self.W(e, sV, ("h1bf", c))
                self.W(e, sC, ("h1sq", c))
                if c == 0:
                    self.W(e, sC, "ln2bc")
                for h in range(2):
                    self.I(lambda c=c, h=h: e.matmul(
                        PC[0:1, h * 512:(h + 1) * 512], self.onesb[:, 0:1],
                        self.z1bfP[c % 2][:, h * 512:(h + 1) * 512],
                        start=(c == 0), stop=(c == ND - 1)))
                for h in range(2):
                    self.I(lambda c=c, h=h: e.matmul(
                        PD[0:1, h * 512:(h + 1) * 512], self.onesb[:, 0:1],
                        self.sqbfP[c % 2][:, h * 512:(h + 1) * 512],
                        start=(c == 0), stop=(c == ND - 1)), sT, ("st4b", c))
        self._ln_rows(w, e, PC, PD, self.rows4, self.nmr_bc4, self.rstd_bc4,
                      pref="ln3", wait_tag=("st4b", ND - 1))
        if w == "v":
            self.W(e, sC, "ln3bc")
            for c in range(ND):
                self.I(lambda c=c: e.tensor_mul(self.tmpa[:, :], self.z1[:, c, :],
                                                self.rstd_bc4[:, :]))
                self.I(lambda c=c: e.tensor_add(self.abuf[:, c * TC:(c + 1) * TC],
                                                self.tmpa[:, :], self.nmr_bc4[:, :]),
                       sV, ("fin", c))
        if w == "s":
            self.W(e, sV, ("h1", ND - 1))
            self.I(lambda: e.dma_start(out=self.h1_d.ap().rearrange("(c p) t -> p c t", p=128),
                                       in_=self.z1[:, :, :]), ds["h1sp"], "h1sp", 16)
            self._tap_after(e, sV, ("h1", ND - 1), "p4h", self.z1[:, :, :])
            self._tap_after(e, sV, ("fin", ND - 1), "p4f", self.abuf[:, :])

        if self.cut(w, e, 4, self.sV, ("fin", ND - 1), self.z1[:, :, :]):
            return

        self.phase = "p5"
        for half in (0, 1):
            ph = "p5a" if half == 0 else "p5c"
            self.phase = ph
            i0 = half * NIH
            if w == "g":
                for i in range(NIH):
                    par = i % 2
                    if i == 0 and half == 0:
                        self.W(e, sV, ("corr", ND - 1))
                        self.W(e, sC, ("gnsq", ND - 1))
                    if i == 0 and half == 1:
                        self.W(e, sT, ("p5apeu", NIH - 1))
                    if i >= 2:
                        self.W(e, sT, (ph + "peu", i - 2))
                    self.I(lambda i=i, par=par, i0=i0: e.dma_start(
                        out=self.trG[par][:, :, :],
                        in_=self.gw_h[(i0 + i) * 128:(i0 + i + 1) * 128, :].rearrange(
                            "p (k j) -> p k j", j=128)),
                        dps("cg", i), None, 16)
                    self.I(lambda i=i, par=par, i0=i0: e.dma_start(
                        out=self.trU[par][:, :, :],
                        in_=self.uw_h[(i0 + i) * 128:(i0 + i + 1) * 128, :].rearrange(
                            "p (k j) -> p k j", j=128)),
                        dps("cg", i), (ph + "cg", i), 16)
            if w == "t":
                for i in range(NIH):
                    par = i % 2
                    psg = PA if par == 0 else PC
                    psu = PB if par == 0 else PD
                    self.W(e, dps("cg", i), (ph + "cg", i))
                    if i == 1 and half == 0:
                        self.W(e, sC, "ln3bc")
                    if i == 0 and half == 1:
                        self.W(e, sC, ("p5b_z2ev", ND - 1))
                    if i >= 2:
                        self.W(e, sV, (ph + "hdn", i - 2))
                    for h in range(2):
                        for d in range(ND):
                            if i == 0 and half == 0 and h == 0:
                                self.W(e, sV, ("fin", d))
                            last = (d == ND - 1 and h == 1)
                            self.I(lambda psg=psg, i=i, d=d, h=h, par=par: e.matmul(
                                psg[:, h * 512:(h + 1) * 512], self.trG[par][:, d, :],
                                self.abuf[:, d * TC + h * 512: d * TC + (h + 1) * 512],
                                start=(d == 0), stop=(d == ND - 1)),
                                sT if last else None, (ph + "peg", i) if last else None)
                    for h in range(2):
                        for d in range(ND):
                            last = (d == ND - 1 and h == 1)
                            self.I(lambda psu=psu, i=i, d=d, h=h, par=par: e.matmul(
                                psu[:, h * 512:(h + 1) * 512], self.trU[par][:, d, :],
                                self.abuf[:, d * TC + h * 512: d * TC + (h + 1) * 512],
                                start=(d == 0), stop=(d == ND - 1)),
                                sT if last else None, (ph + "peu", i) if last else None)
            if w == "a":
                for i in range(NIH):
                    par = i % 2
                    psg = PA if par == 0 else PC
                    psu = PB if par == 0 else PD
                    self.W(e, sT, (ph + "peg", i))
                    if i >= 1:
                        self.W(e, sV, (ph + "hdn", i - 1))
                    self.I(lambda i=i, psg=psg, i0=i0: e.activation(
                        out=self.sgB[:, :], in_=psg[:, :], func=AF.Silu,
                        scale=self.vecs44[:, i0 + i:i0 + i + 1]), sC, (ph + "sg", i))
                    self.W(e, sT, (ph + "peu", i))
                    self.I(lambda i=i, psu=psu, i0=i0: e.activation(
                        out=self.uuB[:, :], in_=psu[:, :], func=AF.Copy,
                        scale=self.vecs44[:, 44 + i0 + i:44 + i0 + i + 1]), sC, (ph + "uu", i))
            if w == "v":
                for i in range(NIH):
                    self.W(e, sC, (ph + "uu", i))
                    if i == 0 and half == 0:
                        self.W(e, sV, ("fin", ND - 1))
                        self.W(e, ds["h1sp"], "h1sp")
                        self.W(e, sT, "ln3bcmm")
                    if i == 0 and half == 1:
                        self.W(e, sT, ("p5b_pez", ND - 1))
                    self.I(lambda i=i: e.tensor_mul(self.hdn1[:, i, :], self.sgB[:, :],
                                                    self.uuB[:, :]), sV, (ph + "hdn", i))
            if w == "s" and half == 0:
                self._tap_after(e, sV, ("p5ahdn", NIH - 1), "p5a", self.hdn1[:, :, :])

            phd = "p5b_" if half == 0 else "p6_"
            self.phase = phd
            if w == "g":
                for c in range(ND):
                    par = c % 2
                    if c == 0:
                        if half == 0:
                            self.W(e, sV, ("corr", ND - 1))
                        else:
                            self.W(e, sT, ("p5b_pez", ND - 1))
                    if c == 1 and half == 0:
                        self.W(e, sT, ("wgw", ND - 1))
                    if c >= 2:
                        self.W(e, sT, (phd + "pez", c - 2))
                    self.I(lambda c=c, par=par, half=half: e.dma_start(
                        out=self.trD[par][:, :, :],
                        in_=self.dw_h[c * 128:(c + 1) * 128,
                                      half * NIH * 128:(half + 1) * NIH * 128].rearrange(
                            "p (k j) -> p k j", j=128)),
                        dps("cd", c), (phd + "cd", c), 16)
            if w == "t":
                self.W(e, sV, (ph + "hdn", NIH - 1))
                for c in range(ND):
                    par = c % 2
                    ps = PA if par == 0 else PB
                    self.W(e, dps("cd", c), (phd + "cd", c))
                    if c == 0:
                        self.W(e, sC, (ph + "uu", NIH - 1))
                    if c >= 2:
                        if half == 0:
                            self.W(e, sC, (phd + "z2ev", c - 2))
                        else:
                            self.W(e, sV, (phd + "z2ac", c - 2))
                    for h in range(2):
                        for i in range(NIH):
                            last = (i == NIH - 1 and h == 1)
                            self.I(lambda ps=ps, c=c, i=i, h=h, par=par: e.matmul(
                                ps[:, h * 512:(h + 1) * 512], self.trD[par][:, i, :],
                                self.hdn1[:, i, h * 512:(h + 1) * 512],
                                start=(i == 0), stop=(i == NIH - 1)),
                                sT if last else None, (phd + "pez", c) if last else None)
                    if half == 1 and c >= 1:
                        self._st6(e, c - 1)
                if half == 1:
                    self._st6(e, ND - 1)
            if half == 0:
                if w == "a":
                    for c in range(ND):
                        par = c % 2
                        ps = PA if par == 0 else PB
                        self.W(e, sT, (phd + "pez", c))
                        if c == 0:
                            self.W(e, ds["h1sp"], "h1sp")
                        self.I(lambda c=c, ps=ps: e.activation(
                            out=self.z1[:, c, :], in_=ps[:, :], func=AF.Copy,
                            scale=self.vecs16[:, 48 + c:49 + c]), sC, (phd + "z2ev", c))
            else:
                if w == "v":
                    for c in range(ND):
                        par = c % 2
                        ps = PA if par == 0 else PB
                        self.W(e, sT, (phd + "pez", c))
                        self.I(lambda c=c, ps=ps: e.scalar_tensor_tensor(
                            self.z1[:, c, :], ps[:, :], self.vecs16[:, 48 + c:49 + c],
                            self.z1[:, c, :], ALU.mult, ALU.add), sV, (phd + "z2ac", c))
                        if c >= 1:
                            self.W(e, sT, ("st6", c - 1))
                        self.I(lambda c=c: e.tensor_copy(self.z2bf[:, :], self.z1[:, c, :]),
                               sV, ("z2bf", c))
                if w == "a":
                    for c in range(ND):
                        self.W(e, sV, (phd + "z2ac", c))
                        if c >= 1:
                            self.W(e, sT, ("st6", c - 1))
                        self.I(lambda c=c: e.activation(
                            out=self.sqbf6[:, :], in_=self.z1[:, c, :], func=AF.Square),
                            sC, ("z2sq", c))
        if w == "s":
            self._tap_after(e, sC, ("p5b_z2ev", ND - 1), "p5b", self.z1[:, :, :])
            self._tap_after(e, sV, ("p6_z2ac", ND - 1), "p6", self.z1[:, :, :])

        if self.cut(w, e, 6, self.sT, ("st6", ND - 1), self.z1[:, :, :]):
            return

        self.phase = "p7"
        self._ln_rows(w, e, PC, PD, self.rows6, self.nmr_bc7, self.rstd_bc7,
                      pref="ln4", wait_tag=("st6", ND - 1))
        if w == "s":
            def h1load(c):
                if c == 0:
                    self.W(e, ds["h1sp"], "h1sp")
                    self.W(e, sT, ("p5cpeu", NIH - 1))
                    self.W(e, sV, ("p5chdn", NIH - 1))
                self.I(lambda c=c: e.dma_start(
                    out=self.h1rl[c % 4][:, :],
                    in_=self.h1_d.ap()[c * 128:(c + 1) * 128, :]),
                    dps("h1r", c), ("h1r", c), 16)
            for c in range(4):
                h1load(c)
            for c in range(ND):
                if c % 2 == 0:
                    self.W(e, sV, ("out", c))
                    self.I(lambda c=c: e.dma_start(
                        out=self.y_h[c * 128:(c + 1) * 128, :],
                        in_=self.outb[c % 4][:, :]), dps("yq", c), ("yst", c), 16)
                if c + 4 < ND:
                    self.W(e, sV, ("hm", c))
                    h1load(c + 4)
            self.W(e, dps("yq", 0), ("yst", ND - 2))
            for _tn in self.taps:
                self.W(e, ds["tapq"], "tap_" + _tn)
        if w == "g":
            for c in range(1, ND, 2):
                self.W(e, sV, ("out", c))
                self.I(lambda c=c: e.dma_start(
                    out=self.y_h[c * 128:(c + 1) * 128, :],
                    in_=self.outb[c % 4][:, :]), dps("yq", c), ("yst", c), 16)
            self.W(e, dps("yq", 1), ("yst", ND - 1))
        if w == "v":
            self.W(e, sC, "ln4bc")
            for c in range(ND):
                self.W(e, dps("h1r", c), ("h1r", c))
                if c >= 4:
                    self.W(e, dps("yq", c - 4), ("yst", c - 4))
                self.I(lambda c=c: e.tensor_mul(self.tmp7a[:, :], self.z1[:, c, :],
                                                self.rstd_bc7[:, :]))
                self.I(lambda c=c: e.tensor_add(self.tmp7b[:, :], self.h1rl[c % 4][:, :],
                                                self.nmr_bc7[:, :]), sV, ("hm", c))
                self.I(lambda c=c: e.tensor_add(self.outb[c % 4][:, :], self.tmp7a[:, :],
                                                self.tmp7b[:, :]), sV, ("out", c))

    def _st6(self, e, c):
        PC, PD = self.PC, self.PD
        self.W(e, self.sV, ("z2bf", c))
        self.W(e, self.sC, ("z2sq", c))
        if c == 0:
            self.W(e, self.sC, ("p5cuu", NIH - 1))
        for h in range(2):
            self.I(lambda c=c, h=h: e.matmul(
                PC[0:1, h * 512:(h + 1) * 512], self.onesb[:, 0:1],
                self.z2bf[:, h * 512:(h + 1) * 512],
                start=(c == 0), stop=(c == ND - 1)))
        for h in range(2):
            self.I(lambda c=c, h=h: e.matmul(
                PD[0:1, h * 512:(h + 1) * 512], self.onesb[:, 0:1],
                self.sqbf6[:, h * 512:(h + 1) * 512],
                start=(c == 0), stop=(c == ND - 1)), self.sT, ("st6", c))

    def _ln_rows(self, w, e, PSm, PSq, rows, nmr_bc, rstd_bc, pref, wait_tag):
        sT, sV, sC = self.sT, self.sV, self.sC
        denom = float(D)
        if w == "a":
            self.W(e, sT, wait_tag)
            self.I(lambda: e.activation(out=rows[0:1, 0:TC], in_=PSm[0:1, 0:TC],
                                        func=AF.Copy, scale=-1.0 / denom))
            self.I(lambda: e.activation(out=rows[0:1, TC:2 * TC], in_=PSq[0:1, 0:TC],
                                        func=AF.Copy, scale=1.0 / denom), sC, pref + "rows")
        if w == "v":
            self.W(e, sC, pref + "rows")
            self.I(lambda: e.tensor_mul(rows[0:1, 2 * TC:3 * TC], rows[0:1, 0:TC],
                                        rows[0:1, 0:TC]))
            self.I(lambda: e.tensor_sub(rows[0:1, TC:2 * TC], rows[0:1, TC:2 * TC],
                                        rows[0:1, 2 * TC:3 * TC]))
            self.I(lambda: e.tensor_scalar_add(rows[0:1, TC:2 * TC], rows[0:1, TC:2 * TC],
                                               EPS), sV, pref + "var")
        if w == "a":
            self.W(e, sV, pref + "var")
            self.I(lambda: e.activation(out=rows[0:1, TC:2 * TC], in_=rows[0:1, TC:2 * TC],
                                        func=AF.Ln))
            self.I(lambda: e.activation(out=rows[0:1, TC:2 * TC], in_=rows[0:1, TC:2 * TC],
                                        func=AF.Exp, scale=-0.5), sC, pref + "rstd")
        if w == "v":
            self.W(e, sC, pref + "rstd")
            self.I(lambda: e.tensor_mul(rows[0:1, 0:TC], rows[0:1, 0:TC],
                                        rows[0:1, TC:2 * TC]), sV, pref + "nmr")
        if w == "t":
            self.W(e, sV, pref + "nmr")
            self.W(e, sV, "o1r")
            for h in range(2):
                self.I(lambda h=h: e.matmul(PSm[:, h * 512:(h + 1) * 512], self.ones1r[:, :],
                                            rows[0:1, h * 512:(h + 1) * 512],
                                            start=True, stop=True))
            for h in range(2):
                self.I(lambda h=h: e.matmul(PSq[:, h * 512:(h + 1) * 512], self.ones1r[:, :],
                                            rows[0:1, TC + h * 512:TC + (h + 1) * 512],
                                            start=True, stop=True), sT, pref + "bcmm")
        if w == "a":
            self.W(e, self.sT, pref + "bcmm")
            self.I(lambda: e.copy(out=nmr_bc[:, :], in_=PSm[:, :]))
            self.I(lambda: e.copy(out=rstd_bc[:, :], in_=PSq[:, :]), sC, pref + "bc")

    def _wo_load(self, e, c):
        dps = self.ps
        if c < 3:
            self.I(lambda c=c: e.dma_start(
                out=self.trWoA[c][:, :, :],
                in_=self.wo_h[c * 128:(c + 1) * 128, :].rearrange(
                    "p (k j) -> p k j", j=128)),
                dps("co", c), ("co", c), 16)
            return
        if c == 3:
            self.W(e, self.sV, ("corr", 11))
        if c >= 5:
            self.W(e, self.sT, ("wgw", c - 2))
        self.I(lambda c=c: e.dma_start(
            out=self.trWo[c % 2][:, :, :],
            in_=self.wo_h[c * 128:(c + 1) * 128, :].rearrange(
                "p (k j) -> p k j", j=128)),
            dps("co", c), ("co", c), 16)

    def _carry_stage(self, e, g):
        self.W(e, self.sV, ("gt", g * 4 + 3))
        self.I(lambda g=g: e.dma_start(out=self.cc_in[g].ap()[:, :], in_=self.gt[:, :]),
               self.dsem["gts"], ("gts", g), 16)
        self.W(e, self.dsem["gts"], ("gts", g))
        if not NO_CC:
            self.I(lambda g=g: e.collective_compute(
                "AllGather", ALU.bypass, replica_groups=[list(range(NCORE))],
                ins=[self.cc_in[g].ap().opt()], outs=[self.cc_out[g].ap().opt()]),
                self.sCC, ("cc", g))

    def _carry_back(self, e, g):
        if NO_CC:
            if g >= 1:
                self.W(e, self.sV, ("cin", g - 1))
            self.I(lambda g=g: e.memset(self.gath[:, :], 0.0),
                   self.dsem["gtb"], ("gtb", g), 16)
            return
        self.W(e, self.sCC, ("cc", g))
        if g >= 1:
            self.W(e, self.sV, ("cin", g - 1))
        self.I(lambda g=g: e.dma_start(
            out=self.gath[:, :].rearrange("p (r j) -> p r j", j=4),
            in_=self.cc_out[g].ap().rearrange("(r p) j -> p r j", p=128)),
            self.dsem["gtb"], ("gtb", g), 16)

    def _tap(self, e, name, view):
        if name not in self.taps:
            return
        self.I(lambda: e.dma_start(out=self.taps[name], in_=view), self.dsem["tapq"], "tap_" + name, 16)

    def _tap_after(self, e, sem, tag, name, view):
        if name not in self.taps:
            return
        self.W(e, sem, tag)
        v = view.rearrange("p a b -> p (a b)") if len(view.shape) == 3 else view
        self.I(lambda: e.dma_start(out=self.taps[name], in_=v), self.dsem["tapq"], "tap_" + name, 16)


def _prep_inputs(L0_attn_x, L0_attn_Wr, L0_attn_Wk, L0_attn_Wv, L0_attn_Wo,
                 L0_attn_decay, L0_attn_gn_weight, L0_attn_gn_bias,
                 L0_ffn_gate_w, L0_ffn_up_w, L0_ffn_down_w,
                 L0_ffn_gate_s, L0_ffn_up_s, L0_ffn_down_s):
    xT = np.ascontiguousarray(np.asarray(L0_attn_x, np.float32)[0].T)

    def tile_lhsT(v):
        w = np.asarray(v, np.float32)
        J2, K2 = w.shape
        return np.ascontiguousarray(
            w.reshape(J2 // 128, 128, K2 // 128, 128).transpose(0, 3, 2, 1)
            .reshape(J2, K2))

    wr = tile_lhsT(L0_attn_Wr)
    wk = tile_lhsT(L0_attn_Wk)
    wv = tile_lhsT(L0_attn_Wv)
    wo = tile_lhsT(L0_attn_Wo)
    gw = tile_lhsT(L0_ffn_gate_w)
    uw = tile_lhsT(L0_ffn_up_w)
    dw = tile_lhsT(L0_ffn_down_w)

    def cols16(v):
        return np.ascontiguousarray(np.asarray(v, np.float32).reshape(16, 128).T)

    vecs16 = np.ascontiguousarray(np.concatenate(
        [cols16(L0_attn_decay), cols16(L0_attn_gn_weight),
         cols16(L0_attn_gn_bias), cols16(L0_ffn_down_s)], axis=1))

    def cols44(v):
        return np.asarray(v, np.float32).reshape(44, 128).T

    vecs44 = np.ascontiguousarray(
        np.concatenate([cols44(L0_ffn_gate_s), cols44(L0_ffn_up_s)], axis=1))

    in_maps = []
    for core in range(NCORE):
        iota = np.broadcast_to(
            np.arange(core * TC, (core + 1) * TC, dtype=np.float32)[None, :],
            (128, TC)).copy()
        cmask = np.zeros((128, 8, 4), np.float32)
        cmask[:, :core, :] = 1.0
        in_maps.append({
            "x": np.ascontiguousarray(xT[:, core * TC:(core + 1) * TC]),
            "wr": wr, "wk": wk, "wv": wv, "wo": wo,
            "gw": gw, "uw": uw, "dw": dw,
            "vecs16": vecs16, "vecs44": vecs44,
            "iota": iota, "cmask": np.ascontiguousarray(cmask.reshape(128, 32)),
        })
    return in_maps


_CACHED = {}


def kernel(**inputs):
    in_maps = _prep_inputs(**inputs)
    key = ("nc", TAPS, LIMIT, NO_CC, NDA, DBG0)
    if key not in _CACHED:
        _CACHED.clear()
        _CACHED[key] = Builder().build()
    nc = _CACHED[key]
    res = run_bass_kernel_spmd(nc, in_maps, core_ids=list(range(NCORE)))
    out = np.empty((T, D), np.float32)
    for core in range(NCORE):
        out[core * TC:(core + 1) * TC, :] = res.results[core]["y"].T
    kernel.last_results = res
    return out.reshape(1, T, D)


# version 19
# speedup vs baseline: 1.0598x; 1.0251x over previous
import numpy as np
import concourse.bass as bass
import concourse.mybir as mybir
from concourse.bass_utils import run_bass_kernel_spmd

F32 = mybir.dt.float32
BF16 = mybir.dt.bfloat16
ALU = mybir.AluOpType
AF = mybir.ActivationFunctionType
AX = mybir.AxisListType.X

D = 2048
ND = 16
II = 5632
NI = 44
NIH = 22
T = 8192
TC = 1024
NCORE = 8
TD = float(T * D)
EPS = 1e-5

TAPS = ()
LIMIT = 99
NO_CC = False
NDA = 16
DBG0 = False

KB = 1024
X0 = 0
Y0 = 64 * KB
Z0 = 128 * KB
ARENA_BYTES = 168 * KB


class Sem:

    def __init__(self, name):
        self.name = name
        self.h = None
        self.val = 0
        self.marks = {}

    def reset(self):
        self.val = 0

    def inc(self, tag=None, n=1):
        self.val += n
        if tag is not None:
            self.marks[tag] = self.val
        return self.val

    def at(self, tag):
        return self.marks[tag]


class Builder:
    def __init__(self):
        nc = bass.Bass("TRN2", target_bir_lowering=False, debug=False)
        self.nc = nc
        self.x_h = nc.dram_tensor("x", [D, TC], F32, kind="ExternalInput").ap()
        self.wr_h = nc.dram_tensor("wr", [D, D], F32, kind="ExternalInput").ap()
        self.wk_h = nc.dram_tensor("wk", [D, D], F32, kind="ExternalInput").ap()
        self.wv_h = nc.dram_tensor("wv", [D, D], F32, kind="ExternalInput").ap()
        self.wo_h = nc.dram_tensor("wo", [D, D], F32, kind="ExternalInput").ap()
        self.gw_h = nc.dram_tensor("gw", [II, D], F32, kind="ExternalInput").ap()
        self.uw_h = nc.dram_tensor("uw", [II, D], F32, kind="ExternalInput").ap()
        self.dw_h = nc.dram_tensor("dw", [D, II], F32, kind="ExternalInput").ap()
        self.vecs16_h = nc.dram_tensor("vecs16", [128, 64], F32, kind="ExternalInput").ap()
        self.vecs44_h = nc.dram_tensor("vecs44", [128, 88], F32, kind="ExternalInput").ap()
        self.iota_h = nc.dram_tensor("iota", [128, TC], F32, kind="ExternalInput").ap()
        self.cmask_h = nc.dram_tensor("cmask", [128, 32], F32, kind="ExternalInput").ap()
        self.y_h = nc.dram_tensor("y", [D, TC], F32, kind="ExternalOutput").ap()

        self.h0_d = nc.dram_tensor("h0d", [D, TC], F32)
        self.h1_d = nc.dram_tensor("h1d", [D, TC], F32)
        self.cc_in = [nc.dram_tensor(f"ccin{g}", [128, 4], F32) for g in range(4)]
        self.cc_out = [nc.dram_tensor(f"ccout{g}", [128 * NCORE, 4], F32) for g in range(4)]
        self.st_in = nc.dram_tensor("stin", [1, 2], F32)
        self.st_out = nc.dram_tensor("stout", [NCORE, 2], F32)

        self.taps = {}
        for t in TAPS:
            shp, dt_ = {
                "p1": ([128, ND * TC], BF16),
                "p2": ([128, ND * TC], F32),
                "p3": ([128, ND * TC], BF16),
                "p4h": ([128, ND * TC], F32),
                "p4z": ([128, ND * TC], F32),
                "p4w": ([128, 32], F32),
                "p4b": ([128, 16], F32),
                "p4f": ([128, ND * TC], BF16),
                "p5a": ([128, NIH * TC], BF16),
                "p5b": ([128, ND * TC], F32),
                "p6": ([128, ND * TC], F32),
            }[t]
            self.taps[t] = nc.dram_tensor("tap_" + t, shp, dt_, kind="ExternalOutput").ap()
        if DBG0:
            self.att0 = nc.dram_tensor("att0", [128, 4 * TC], F32, kind="ExternalOutput").ap()

        self.sT = Sem("sT")
        self.sV = Sem("sV")
        self.sC = Sem("sC")
        self.sCC = Sem("sCC")
        names = ["cst", "gts", "gtb", "gn", "xsp", "h1sp", "tapq",
                 "xld0", "xld1", "xld2", "xld3",
                 "ct0", "ct1", "ct2", "ct3"]
        for pn in ("cw", "co", "cg", "cd", "h0", "h1r", "yq"):
            names += [pn + "0", pn + "1"]
        self.dsem = {n: Sem(n) for n in names}


    def ps(self, base, idx):
        return self.dsem[base + str(idx % 2)]

    def cut(self, w, e, n, sem, tag, view):
        if LIMIT > n:
            return False
        if w == "s":
            self.W(e, sem, tag)
            self.I(lambda: e.dma_start(
                out=self.y_h.rearrange("(c p) t -> p c t", p=128), in_=view),
                self.dsem["yq0"], "ycut", 16)
            self.W(e, self.dsem["yq0"], "ycut")
        return True
    def carve(self, off, p, shape_free_elems, dt):
        esz = 2 if dt == BF16 else 4
        nb = int(np.prod(shape_free_elems)) * esz
        assert off + nb <= ARENA_BYTES, (off, nb)
        ap = self.arena.ap()[0:p, off:off + nb].bitcast(dt)
        if len(shape_free_elems) == 2:
            a, b = shape_free_elems
            ap = ap.rearrange("p (a b) -> p a b", b=b)
        return ap

    def build(self):
        nc = self.nc
        import contextlib
        es = contextlib.ExitStack()
        self.arena = es.enter_context(nc.sbuf_tensor("arena", [128, ARENA_BYTES], mybir.dt.uint8))
        self.abuf = es.enter_context(nc.sbuf_tensor("abuf", [128, ND * TC], BF16))
        self.iota = es.enter_context(nc.sbuf_tensor("iota_s", [128, TC], F32))
        self.vecs16 = es.enter_context(nc.sbuf_tensor("v16", [128, 64], F32))
        self.vecs44 = es.enter_context(nc.sbuf_tensor("v44", [128, 88], F32))
        self.ldc = es.enter_context(nc.sbuf_tensor("ldc", [128, 16], F32))
        self.nldc = es.enter_context(nc.sbuf_tensor("nldc", [128, 16], F32))
        self.dnsb = es.enter_context(nc.sbuf_tensor("dnsb", [128, 16], BF16))
        self.gsb = es.enter_context(nc.sbuf_tensor("gsb", [128, 64], F32))
        self.cmask = es.enter_context(nc.sbuf_tensor("cmask_s", [128, 32], F32))
        self.S1 = es.enter_context(nc.sbuf_tensor("S1", [128, 16], F32))
        self.S2 = es.enter_context(nc.sbuf_tensor("S2", [128, 16], F32))
        self.gt = es.enter_context(nc.sbuf_tensor("gt", [128, 4], F32))
        self.cin = es.enter_context(nc.sbuf_tensor("cin", [128, 4], F32))
        self.gath = es.enter_context(nc.sbuf_tensor("gath", [128, 32], F32))
        self.mm_sc = es.enter_context(nc.sbuf_tensor("mmsc", [128, 32], F32))
        self.onesb = es.enter_context(nc.sbuf_tensor("onesb", [128, 1], BF16))
        self.onesf = es.enter_context(nc.sbuf_tensor("onesf", [128, 1], F32))
        self.ones1r = es.enter_context(nc.sbuf_tensor("ones1r", [1, 128], F32))
        self.strow = es.enter_context(nc.sbuf_tensor("strow", [1, 2], F32))
        self.stg = es.enter_context(nc.sbuf_tensor("stg", [1, 16], F32))
        self.tot2 = es.enter_context(nc.sbuf_tensor("tot2", [1, 8], F32))
        self.mr_bc = es.enter_context(nc.sbuf_tensor("mrbc", [128, 2], F32))
        self.Agn = es.enter_context(nc.sbuf_tensor("Agn", [128, 16], F32))
        self.Bgn = es.enter_context(nc.sbuf_tensor("Bgn", [128, 16], F32))
        self.pk2 = es.enter_context(nc.sbuf_tensor("pk2", [128, 2], F32))
        self.gbw = es.enter_context(nc.sbuf_tensor("gbw", [128, 32], BF16))
        self.wgwsb = es.enter_context(nc.sbuf_tensor("wgwsb", [128, 32], F32))
        self.w2sb = es.enter_context(nc.sbuf_tensor("w2sb", [128, 16], F32))
        self.s2c = es.enter_context(nc.sbuf_tensor("s2c", [128, 1], F32))

        self.PA = es.enter_context(nc.psum_tensor("PA", [128, 1024], F32))
        self.PB = es.enter_context(nc.psum_tensor("PB", [128, 1024], F32))
        self.PC = es.enter_context(nc.psum_tensor("PC", [128, 1024], F32))
        self.PD = es.enter_context(nc.psum_tensor("PD", [128, 1024], F32))

        for s in self._all_sems():
            s.h = es.enter_context(nc.semaphore(s.name))

        cv = self.carve
        self.xres = cv(X0, 128, (ND, TC), F32)
        self.hdn1 = cv(X0, 128, (NIH, TC), BF16)
        pass
        self.z2bf = cv(X0 + 44 * KB, 128, (TC,), BF16)
        self.sqbf6 = cv(X0 + 46 * KB, 128, (TC,), BF16)
        self.rows6 = cv(X0 + 48 * KB, 1, (3 * TC,), F32)
        self.nmr_bc7 = cv(X0 + 60 * KB, 128, (TC,), F32)
        self.rstd_bc7 = cv(X0 + 44 * KB, 128, (TC,), F32)
        self.tmp7a = cv(X0 + 48 * KB, 128, (TC,), F32)
        self.tmp7b = cv(X0 + 52 * KB, 128, (TC,), F32)
        self.sqbf1 = cv(Z0 + 0 * KB, 128, (TC,), BF16)
        self.rows1 = cv(Z0 + 2 * KB, 1, (3 * TC,), F32)
        self.nmr_bc1 = cv(Z0 + 14 * KB, 128, (TC,), F32)
        self.rstd_bc1 = cv(Z0 + 18 * KB, 128, (TC,), F32)
        self.tmp1 = cv(Z0 + 22 * KB, 128, (TC,), F32)
        off = Y0 + 24 * KB
        self.trW = {}
        for w in ("r", "k", "v"):
            self.trW[w] = [cv(off + i * 4 * KB, 128, (ND, 128), BF16) for i in range(2)]
            off += 8 * KB
        self.kbuf = cv(Y0 + 48 * KB, 128, (TC,), F32)
        self.vbuf = cv(Y0 + 52 * KB, 128, (TC,), F32)
        self.scbuf = cv(Y0 + 56 * KB, 128, (TC,), F32)
        self.s0 = cv(Y0 + 60 * KB, 128, (TC,), F32)
        self.z1 = cv(Y0, 128, (ND, TC), F32)
        self.xbfv = [cv(X0 + c * 4 * KB, 128, (TC,), BF16) for c in range(ND)]
        self.trWoA = [cv(Y0 + 40 * KB, 128, (ND, 128), BF16), cv(Y0 + 44 * KB, 128, (ND, 128), BF16),
                      cv(Y0 + 60 * KB, 128, (ND, 128), BF16)]
        self.s1b = cv(Z0 + 0 * KB, 128, (TC,), F32)
        self.Bring = [cv(Z0 + 4 * KB + i * 4 * KB, 128, (TC,), F32) for i in range(8)]
        self.sqdump = cv(Z0 + 36 * KB, 128, (TC,), BF16)
        self.trWo = [cv(Z0 + 8 * KB + i * 4 * KB, 128, (ND, 128), BF16) for i in range(2)]
        self.trWoT = [cv(X0 + off * KB, 128, (4, 128), BF16)
                      for off in (54, 55, 58, 59)]
        self.trG = [cv(Z0 + 16 * KB + i * 4 * KB, 128, (ND, 128), BF16) for i in range(2)]
        self.trU = [cv(Z0 + 24 * KB + i * 4 * KB, 128, (ND, 128), BF16) for i in range(2)]
        self.sgB = cv(Z0 + 32 * KB, 128, (TC,), F32)
        self.uuB = cv(Z0 + 36 * KB, 128, (TC,), F32)
        self.trD = [cv(Z0 + 0 * KB + i * 6 * KB, 128, (NIH, 128), BF16) for i in range(2)]
        self.h1rl = [cv(Z0 + 24 * KB + i * 4 * KB, 128, (TC,), F32) for i in range(4)]
        self.outb = [cv(Z0 + 0 * KB + i * 4 * KB, 128, (TC,), F32) for i in range(4)]

        ab = self.abuf.ap()
        self.z1bfP = [ab[:, 14336:15360], ab[:, 15360:16384]]
        self.sqbfP = [ab[:, 12288:13312], ab[:, 13312:14336]]
        self.h0rl = [ab[:, 8192:10240].bitcast(F32), ab[:, 10240:12288].bitcast(F32)]
        self.tmpb = ab[:, 4096:6144].bitcast(F32)
        self.tmpa = cv(Z0 + 36 * KB, 128, (TC,), F32)
        self.rows4 = cv(X0 + 52 * KB, 1, (3 * TC,), F32)
        self.nmr_bc4 = cv(X0 + 44 * KB, 128, (TC,), F32)
        self.rstd_bc4 = cv(X0 + 48 * KB, 128, (TC,), F32)

        self.plan = True
        for w in ("g", "s", "a", "t", "v"):
            for s in self._all_sems():
                s.reset()
            self.stream(None, w)
        self.plan = False
        with nc.Block() as block:
            @block.gpsimd
            def _(g):
                self._emit(g, "g")

            @block.sync
            def _(s):
                self._emit(s, "s")

            @block.scalar
            def _(a):
                self._emit(a, "a")

            @block.tensor
            def _(t):
                self._emit(t, "t")

            @block.vector
            def _(v):
                self._emit(v, "v")
        es.close()
        return nc

    def _all_sems(self):
        return [self.sT, self.sV, self.sC, self.sCC] + list(self.dsem.values())

    def _emit(self, eng, which):
        for s in self._all_sems():
            s.reset()
        self.stream(eng, which)

    def I(self, fn, sem=None, tag=None, n=1):
        if not self.plan:
            with self.nc.named_scope(getattr(self, "phase", "p")):
                inst = fn()
            if sem is not None:
                inst.then_inc(sem.h, n)
        if sem is not None:
            sem.inc(tag, n)

    def W(self, eng, sem, tag):
        if self.plan:
            return
        eng.wait_ge(sem.h, sem.at(tag))

    def stream(self, e, w):
        PA, PB, PC, PD = self.PA, self.PB, self.PC, self.PD
        sT, sV, sC, sCC = self.sT, self.sV, self.sC, self.sCC
        ds = self.dsem
        dps = self.ps

        self.phase = "p0"
        if w in ("s", "g", "a"):
            for q in {"s": (0,), "g": (1,), "a": (2, 3)}[w]:
                self.I(lambda q=q: e.dma_start(
                    out=self.xres[:, 4 * q:4 * q + 4, :],
                    in_=self.x_h.rearrange("(c p) t -> p c t", p=128)[:, 4 * q:4 * q + 4, :]),
                    ds["xld" + str(q)], "xld", 16)
        if w == "g":
            self.I(lambda: e.dma_start(out=self.iota[:, :], in_=self.iota_h[:, :]), ds["cst"], None, 16)
            self.I(lambda: e.dma_start(out=self.vecs16[:, :], in_=self.vecs16_h[:, :]), ds["cst"], None, 16)
            self.I(lambda: e.dma_start(out=self.vecs44[:, :], in_=self.vecs44_h[:, :]), ds["cst"], None, 16)
            self.I(lambda: e.dma_start(out=self.cmask[:, :], in_=self.cmask_h[:, :]), ds["cst"], "cst", 16)
        if w == "a":
            self.W(e, ds["cst"], "cst")
            self.I(lambda: e.activation(out=self.gsb[:, 0:16], in_=self.vecs16[:, 0:16],
                                        func=AF.Exp, scale=-1.0), sC, "dexp")
            self.W(e, sV, "drec")
            self.I(lambda: e.activation(out=self.ldc[:, :], in_=self.gsb[:, 16:32],
                                        func=AF.Ln), sC, "dln")
        if w == "v":
            self.I(lambda: e.memset(self.onesb[:, :], 1.0), sV, "ob")
            self.I(lambda: e.memset(self.onesf[:, :], 1.0), sV, "of")
            self.I(lambda: e.memset(self.ones1r[:, :], 1.0), sV, "o1r")
            self.W(e, sC, "dexp")
            self.I(lambda: e.tensor_scalar_add(self.gsb[:, 16:32], self.gsb[:, 0:16], 1.0),
                   sV, "dadd")
            self.W(e, sV, "dadd")
            self.I(lambda: e.reciprocal(self.gsb[:, 32:48], self.gsb[:, 16:32]), sV, "drcp")
            self.W(e, sV, "drcp")
            self.I(lambda: e.tensor_scalar_max(self.gsb[:, 16:32], self.gsb[:, 32:48], 1e-7),
                   sV, "drec")
            self.W(e, sC, "dln")
            self.I(lambda: e.tensor_scalar_mul(self.nldc[:, :], self.ldc[:, :], -1.0), sV, "dnld")
            self.W(e, ds["cst"], "cst")
            self.I(lambda: e.tensor_copy(self.dnsb[:, :], self.vecs16[:, 48:64]), sV, "dnsb")
            self.I(lambda: e.tensor_copy(
                self.gbw[:, :].rearrange("p (d two) -> p d two", two=2)[:, :, 0:1],
                self.vecs16[:, 32:48].rearrange("p (d two) -> p d two", two=1)), sV, "gbw1")
            self.I(lambda: e.tensor_copy(
                self.gbw[:, :].rearrange("p (d two) -> p d two", two=2)[:, :, 1:2],
                self.vecs16[:, 16:32].rearrange("p (d two) -> p d two", two=1)), sV, "gbw")

        self.phase = "p1"
        if w == "v":
            for c in range(ND):
                if c % 4 == 0:
                    self.W(e, ds["xld" + str(c // 4)], "xld")
                self.I(lambda c=c: e.tensor_copy(self.abuf[:, c * TC:(c + 1) * TC],
                                                 self.xres[:, c, :]), sV, ("xbf", c))
        if w == "a":
            for c in range(ND):
                if c % 4 == 0:
                    self.W(e, ds["xld" + str(c // 4)], "xld")
                if c >= 1:
                    self.W(e, sT, ("st1", c - 1))
                self.I(lambda c=c: e.activation(out=self.sqbf1[:, :], in_=self.xres[:, c, :],
                                                func=AF.Square), sC, ("xsq", c))
        if w == "t":
            self.W(e, sV, "ob")
            for c in range(ND):
                self.W(e, sV, ("xbf", c))
                self.W(e, sC, ("xsq", c))
                for h in range(2):
                    self.I(lambda c=c, h=h: e.matmul(
                        PC[0:1, h * 512:(h + 1) * 512], self.onesb[:, 0:1],
                        self.abuf[:, c * TC + h * 512: c * TC + (h + 1) * 512],
                        start=(c == 0), stop=(c == ND - 1)))
                for h in range(2):
                    self.I(lambda c=c, h=h: e.matmul(
                        PD[0:1, h * 512:(h + 1) * 512], self.onesb[:, 0:1],
                        self.sqbf1[:, h * 512:(h + 1) * 512],
                        start=(c == 0), stop=(c == ND - 1)), sT, ("st1", c))
        self._ln_rows(w, e, PC, PD, self.rows1, self.nmr_bc1, self.rstd_bc1,
                      pref="ln1", wait_tag=("st1", ND - 1))
        if w == "v":
            self.W(e, sC, "ln1bc")
            for c in range(ND):
                self.I(lambda c=c: e.tensor_mul(self.tmp1[:, :], self.xres[:, c, :],
                                                self.rstd_bc1[:, :]))
                self.I(lambda c=c: e.tensor_add(self.abuf[:, c * TC:(c + 1) * TC],
                                                self.tmp1[:, :], self.nmr_bc1[:, :]),
                       sV, ("ain", c))
        
        if self.cut(w, e, 1, self.sV, ("ain", ND - 1), self.xres[:, :, :]):
            return

        self.phase = "p2"
        G = NDA // 4
        if w == "g":
            for c in range(NDA):
                par = c % 2
                if c >= 2:
                    self.W(e, sT, ("pe_r", c - 2))
                for wn, hsrc in (("r", self.wr_h), ("k", self.wk_h), ("v", self.wv_h)):
                    self.I(lambda wn=wn, hsrc=hsrc, c=c, par=par: e.dma_start(
                        out=self.trW[wn][par][:, :, :],
                        in_=hsrc[c * 128:(c + 1) * 128, :].rearrange(
                            "p (k j) -> p k j", j=128)), dps("cw", c),
                        ("cw", c) if wn == "v" else None, 16)
                if DBG0 and c == 0:
                    self.W(e, sC, ("vev", 0))
                    self.I(lambda: e.dma_start(out=self.att0[:, 0:TC], in_=self.kbuf[:, :]),
                           ds["tapq"], "d0a", 16)
                    self.I(lambda: e.dma_start(out=self.att0[:, TC:2 * TC], in_=self.vbuf[:, :]),
                           ds["tapq"], "d0b", 16)
                    self.W(e, sV, ("A", 0))
                    self.I(lambda: e.dma_start(out=self.att0[:, 2 * TC:3 * TC], in_=self.s1b[:, :]),
                           ds["tapq"], "d0c", 16)
                    self.I(lambda: e.dma_start(out=self.att0[:, 3 * TC:4 * TC], in_=self.Bring[0][:, :]),
                           ds["tapq"], "d0d", 16)
                for g in range(G):
                    if 4 * g + 5 == c:
                        self._carry_stage(e, g)
                    if 4 * g + 7 == c:
                        self._carry_back(e, g)
            for g in range(G):
                if 4 * g + 5 >= NDA:
                    self._carry_stage(e, g)
            for c in range(5):
                self._wo_load(e, c)
            for g in range(G):
                if 4 * g + 7 > NDA - 1:
                    self._carry_back(e, g)
        if w == "s":
            self.W(e, sV, ("ain", ND - 1))
            self.I(lambda: e.dma_start(
                out=self.h0_d.ap().rearrange("(c p) t -> p c t", p=128),
                in_=self.xres[:, :, :]), ds["xsp"], "xsp", 16)
            self._tap(e, "p1", self.abuf[:, :])
        if w == "t":
            for c in range(NDA):
                par = c % 2
                self.W(e, dps("cw", c), ("cw", c))
                for wn, ps in (("k", PA), ("v", PB), ("r", PC)):
                    if c >= 1:
                        self.W(e, sC, ({"k": "kev", "v": "vev", "r": "rexp"}[wn], c - 1))
                    for h in range(2):
                        for d in range(ND):
                            if c == 0 and wn == "k" and h == 0:
                                self.W(e, sV, ("ain", d))
                            last = (d == ND - 1 and h == 1)
                            self.I(lambda wn=wn, ps=ps, c=c, d=d, h=h, par=par: e.matmul(
                                ps[:, h * 512:(h + 1) * 512],
                                self.trW[wn][par][:, d, :],
                                self.abuf[:, d * TC + h * 512: d * TC + (h + 1) * 512],
                                start=(d == 0), stop=(d == ND - 1)),
                                sT if last else None, ("pe_" + wn, c) if last else None)
        if w == "a":
            self.W(e, ds["cst"], "cst")
            self.W(e, sV, "dnld")
            for c in range(NDA):
                par = c % 2
                self.W(e, sT, ("pe_k", c))
                if c >= 1:
                    self.W(e, sV, ("kvs", c - 1))
                self.I(lambda c=c: e.copy(out=self.kbuf[:, :], in_=PA[:, :]), sC, ("kev", c))
                self.W(e, sT, ("pe_v", c))
                self.I(lambda c=c: e.copy(out=self.vbuf[:, :], in_=PB[:, :]), sC, ("vev", c))
                self.W(e, sT, ("pe_r", c))
                if DBG0 and c == 1:
                    self.W(e, ds["tapq"], "d0d")
                if c >= 1:
                    self.W(e, sV, ("A", c - 1))
                self.I(lambda c=c: e.activation(out=self.s1b[:, :], in_=PC[:, :],
                                                func=AF.Tanh, scale=0.5), sC, ("rexp", c))
                self.I(lambda c=c: e.activation(out=self.scbuf[:, :], in_=self.iota[:, :],
                                                func=AF.Exp, scale=self.ldc[:, c:c + 1]),
                       sC, ("scl", c))
                self.W(e, sV, ("inva", c))
                self.I(lambda c=c: e.activation(out=self.s0[:, :], in_=self.s0[:, :],
                                                func=AF.Exp), sC, ("inv", c))
                glist = []
                if c % 4 == 3 and c >= 7 and c < NDA - 1:
                    glist.append(c // 4 - 1)
                if c == NDA - 1:
                    glist += [g for g in range(NDA // 4 - 1) if 4 * g + 7 >= NDA - 1]
                for gidx in glist:
                    for cc in range(gidx * 4, gidx * 4 + 4):
                        self.W(e, sV, ("corr", cc))
                        self.I(lambda cc=cc: e.activation(
                            out=self.sqdump[:, :], in_=self.xres[:, cc, :],
                            func=AF.Square, accum_out=self.S2[:, cc:cc + 1]),
                            sC, ("gnsq", cc))
        if w == "v":
            for c in range(NDA):
                bslot = c % 8
                if c == 0:
                    self.W(e, ds["cst"], "cst")
                if DBG0 and c == 1:
                    self.W(e, ds["tapq"], "d0d")
                self.W(e, sC, ("scl", c))
                self.I(lambda c=c, bslot=bslot: e.scalar_tensor_tensor(
                    self.Bring[bslot][:, :], self.s1b[:, :], 1.0, self.scbuf[:, :],
                    ALU.add, ALU.mult), sV, ("rr", c))
                self.I(lambda c=c: e.tensor_scalar(
                    self.s0[:, :], self.iota[:, :], self.nldc[:, c:c + 1], 23.0258509299,
                    ALU.mult, ALU.min), sV, ("inva", c))
                if DBG0 and c == 0:
                    self.W(e, ds["tapq"], "d0b")
                self.W(e, sC, ("inv", c))
                self.I(lambda c=c: e.tensor_mul(self.kbuf[:, :], self.kbuf[:, :],
                                                self.vbuf[:, :]), sV, ("kv", c))
                self.I(lambda c=c: e.scalar_tensor_tensor(
                    self.s0[:, :], self.s0[:, :], 1e10, self.kbuf[:, :],
                    ALU.min, ALU.mult), sV, ("kvs", c))
                self.I(lambda c=c: e.tensor_tensor_scan(
                    self.s1b[:, :], self.s0[:, :], self.s0[:, :], 0.0,
                    ALU.add, ALU.bypass), sV, ("scan", c))
                if self.plan:
                    self.sV.marks[("rs", c)] = self.sV.val
                if c == 0:
                    self.W(e, ds["xsp"], "xsp")
                self.I(lambda c=c, bslot=bslot: e.tensor_mul(
                    self.xres[:, c, :], self.s1b[:, :], self.Bring[bslot][:, :]),
                    sV, ("A", c))
                if c >= 4:
                    self.W(e, ds["gts"], ("gts", c // 4 - 1))
                self.I(lambda c=c: e.tensor_copy(self.gt[:, c % 4:c % 4 + 1],
                                                 self.s1b[:, 1023:1024]), sV, ("gt", c))
                glist = []
                if c % 4 == 3 and c >= 7 and c < NDA - 1:
                    glist.append(c // 4 - 1)
                if c == NDA - 1:
                    glist += [g for g in range(G) if 4 * g + 7 >= NDA - 1]
                for gidx in glist:
                    self.W(e, ds["gtb"], ("gtb", gidx))
                    if gidx == 0:
                        self.W(e, ds["cst"], "cst")
                    self.I(lambda: e.tensor_mul(self.mm_sc[:, :], self.gath[:, :],
                                                self.cmask[:, :]), sV, ("mmg", gidx))
                    self.W(e, sV, ("mmg", gidx))
                    self.I(lambda g=gidx: e.tensor_reduce(
                        self.cin[:, :],
                        self.mm_sc[:, :].rearrange("p (r j) -> p j r", j=4),
                        AX, ALU.add), sV, ("cin", gidx))
                    self.W(e, sV, ("cin", gidx))
                    for cc in range(gidx * 4, gidx * 4 + 4):
                        self.I(lambda cc=cc: e.scalar_tensor_tensor(
                            self.xres[:, cc, :], self.Bring[cc % 8][:, :],
                            self.cin[:, cc % 4:cc % 4 + 1], self.xres[:, cc, :],
                            ALU.mult, ALU.add, accum_out=self.S1[:, cc:cc + 1]),
                            sV, ("corr", cc))
                    for cc in range(gidx * 4, gidx * 4 + 4):
                        self.W(e, sC, ("gnsq", cc))
                        self.I(lambda cc=cc: e.tensor_scalar(
                            self.xbfv[cc][:, :], self.xres[:, cc, :],
                            self.vecs16[:, 16 + cc:17 + cc], None, ALU.mult),
                            sV, ("og", cc))
            if self.plan:
                for c in range(NDA):
                    g = c // 4
                    self.sV.marks[("corr", c)] = self.sV.marks[("corr", g * 4 + 3)]
                self.sV.marks[("corr", ND - 1)] = self.sV.marks[("corr", NDA - 1)]
                self.sC.marks = self.sC.marks

        if self.cut(w, e, 2, self.sV, ("corr", NDA - 1), self.xres[:, :, :]):
            return

        self.phase = "p3"
        if w == "v":
            if self.plan:
                self.sV.marks[("norm", ND - 1)] = self.sV.marks[("og", ND - 1)]
            self.W(e, sC, ("gnsq", ND - 1))
            self.I(lambda: e.tensor_reduce(self.pk2[:, 0:1],
                                           self.S1[:, :].rearrange("p (a c) -> p a c", a=1), AX, ALU.add), sV, "gnp1")
            self.W(e, sV, "gnp1")
            self.I(lambda: e.tensor_reduce(self.pk2[:, 1:2],
                                           self.S2[:, :].rearrange("p (a c) -> p a c", a=1), AX, ALU.add),
                   sV, "gnpk")
        if w == "v":
            self.W(e, ds["gn"], "gnbk")
            self.I(lambda: e.tensor_reduce(self.tot2[0:1, 0:2],
                                           self.stg[0:1, :].rearrange("p (r q) -> p q r", q=2),
                                           AX, ALU.add), sV, "gn1")
            self.W(e, sV, "gn1")
            self.I(lambda: e.tensor_scalar_mul(self.tot2[0:1, 2:3], self.tot2[0:1, 0:1],
                                               -1.0 / TD), sV, "gn2")
            self.I(lambda: e.tensor_scalar_mul(self.tot2[0:1, 3:4], self.tot2[0:1, 1:2],
                                               1.0 / TD), sV, "gn3")
            self.W(e, sV, "gn3")
            self.I(lambda: e.tensor_mul(self.tot2[0:1, 4:5], self.tot2[0:1, 2:3],
                                        self.tot2[0:1, 2:3]), sV, "gn4")
            self.W(e, sV, "gn4")
            self.I(lambda: e.tensor_sub(self.tot2[0:1, 5:6], self.tot2[0:1, 3:4],
                                        self.tot2[0:1, 4:5]), sV, "gn5")
            self.W(e, sV, "gn5")
            self.I(lambda: e.tensor_scalar_add(self.tot2[0:1, 5:6], self.tot2[0:1, 5:6],
                                               EPS), sV, "gnvar")
        if w == "v":
            self.W(e, sC, "gnrstd")
            self.I(lambda: e.tensor_copy(self.tot2[0:1, 3:4], self.tot2[0:1, 6:7]), sV, "gnpair")
        if w == "s":
            self._tap_after(e, sV, ("corr", 15), "p2", self.xres[:, :, :])
            self._tap_after(e, sV, ("norm", 15), "p3", self.abuf[:, :])

        if self.cut(w, e, 3, self.sV, ("norm", ND - 1), self.xres[:, :, :]):
            return

        self.phase = "p4"
        if w == "g":
            for c in range(5, ND):
                self._wo_load(e, c)
                if c == 8:
                    self.W(e, sC, "gnrow")
                    self.I(lambda: e.dma_start(out=self.st_in.ap()[:, :],
                                               in_=self.strow[:, :]), ds["gn"], "gnst", 16)
                    self.W(e, ds["gn"], "gnst")
                    self.I(lambda: e.collective_compute(
                        "AllGather", ALU.bypass, replica_groups=[list(range(NCORE))],
                        ins=[self.st_in.ap().opt()], outs=[self.st_out.ap().opt()]),
                        sCC, "gncc")
                    self.W(e, sCC, "gncc")
                    self.I(lambda: e.dma_start(
                        out=self.stg[:, :],
                        in_=self.st_out.ap().rearrange("(a r) q -> a (r q)", a=1)),
                        ds["gn"], "gnbk", 16)
            for c in range(ND):
                if c == 0:
                    self.W(e, sV, ("og", ND - 1))
                    self.W(e, sC, ("gnsq", ND - 1))
                elif c >= 4:
                    self.W(e, sT, ("pe_z1", c - 4))
                self.I(lambda c=c: e.dma_start(
                    out=self.trWoT[c % 4][:, :, :],
                    in_=self.wo_h[c * 128:(c + 1) * 128, 1536:2048].rearrange(
                        "p (k j) -> p k j", j=128)),
                    ds["ct" + str(c % 4)], ("ct", c), 16)
        if w == "t":
            self.W(e, sV, "gbw")
            self.W(e, sC, ("kev", NDA - 1))
            self.W(e, sC, ("vev", NDA - 1))
            self.W(e, sV, ("og", 11))
            for c in range(ND):
                par = c % 2
                ps = PA if par == 0 else PB
                trw = (lambda cc: self.trWoA[cc]) if c < 3 else (lambda cc: self.trWo[par])
                self.W(e, dps("co", c), ("co", c))
                if c >= 2:
                    self.W(e, sC, ("pAev", c - 2))
                if c == 6:
                    self.W(e, sV, "gnpk")
                    self.W(e, sV, "of")
                    self.I(lambda: e.matmul(PD[0:1, 1000:1002], self.onesf[:, 0:1],
                                            self.pk2[:, :], start=True, stop=True),
                           sT, "gnmm")
                for h in range(2):
                    for d in range(12):
                        last = (d == 11 and h == 1)
                        self.I(lambda ps=ps, c=c, d=d, h=h, trw=trw: e.matmul(
                            ps[:, h * 512:(h + 1) * 512],
                            trw(c)[:, d, :],
                            self.xbfv[d][:, h * 512:(h + 1) * 512],
                            start=(d == 0), stop=(d == 11)),
                            sT if last else None, ("pe_a", c) if last else None)
                for d in range(ND):
                    self.I(lambda c=c, d=d, trw=trw: e.matmul(
                        PD[:, 2 * c:2 * c + 2], trw(c)[:, d, :],
                        self.gbw[:, 2 * d:2 * d + 2],
                        start=(d == 0), stop=(d == ND - 1)),
                        sT if d == ND - 1 else None, ("wgw", c) if d == ND - 1 else None)
            self.W(e, sV, "gnpair")
            self.I(lambda: e.matmul(PC[:, 0:2], self.ones1r[:, :],
                                    self.tot2[0:1, 2:4],
                                    start=True, stop=True), sT, "gnbc")
            self.W(e, sV, ("og", ND - 1))
            for c in range(ND):
                par = c % 2
                ps = PA if par == 0 else PB
                self.W(e, ds["ct" + str(c % 4)], ("ct", c))
                if c < 2:
                    self.W(e, sC, ("pAev", ND - 2 + c))
                else:
                    self.W(e, sV, ("zadd", c - 2))
                for h in range(2):
                    for d in range(12, 16):
                        last = (d == 15 and h == 1)
                        self.I(lambda ps=ps, c=c, d=d, h=h: e.matmul(
                            ps[:, h * 512:(h + 1) * 512],
                            self.trWoT[c % 4][:, d - 12, :],
                            self.xbfv[d][:, h * 512:(h + 1) * 512],
                            start=(d == 12), stop=(d == 15)),
                            sT if last else None, ("pe_z1", c) if last else None)
            for c in range(ND):
                self.W(e, sV, ("z1bf", c))
                self.W(e, sC, ("z1sq", c))
                if c == 0:
                    self.W(e, sC, ("wgev", ND - 1))
                    self.W(e, sC, "gnmr")
                for h in range(2):
                    self.I(lambda c=c, h=h: e.matmul(
                        PC[0:1, h * 512:(h + 1) * 512], self.onesb[:, 0:1],
                        self.z1bfP[c % 2][:, h * 512:(h + 1) * 512],
                        start=(c == 0), stop=(c == ND - 1)))
                for h in range(2):
                    self.I(lambda c=c, h=h: e.matmul(
                        PD[0:1, h * 512:(h + 1) * 512], self.onesb[:, 0:1],
                        self.sqbfP[c % 2][:, h * 512:(h + 1) * 512],
                        start=(c == 0), stop=(c == ND - 1)), sT, ("st4", c))
        if w == "a":
            for c in range(ND):
                par = c % 2
                ps = PA if par == 0 else PB
                if c == 4:
                    for cc in range(ND - 4, ND):
                        self.W(e, sV, ("corr", cc))
                        self.I(lambda cc=cc: e.activation(
                            out=self.sqdump[:, :], in_=self.xres[:, cc, :],
                            func=AF.Square, accum_out=self.S2[:, cc:cc + 1]),
                            sC, ("gnsq", cc))
                self.W(e, sT, ("pe_a", c))
                self.I(lambda c=c, ps=ps: e.copy(out=self.z1[:, c, :], in_=ps[:, :]),
                       sC, ("pAev", c))
                if c == 6:
                    self.W(e, sT, "gnmm")
                    self.I(lambda: e.copy(out=self.strow[:, :], in_=PD[0:1, 1000:1002]),
                           sC, "gnrow")
                if c == 10:
                    self.W(e, sV, "gnvar")
                    self.I(lambda: e.activation(out=self.tot2[0:1, 5:6],
                                                in_=self.tot2[0:1, 5:6],
                                                func=AF.Ln), sC, "gnln")
                    self.W(e, sC, "gnln")
                    self.I(lambda: e.activation(out=self.tot2[0:1, 6:7],
                                                in_=self.tot2[0:1, 5:6],
                                                func=AF.Exp, scale=-0.5), sC, "gnrstd")
            self.W(e, sT, "gnbc")
            self.I(lambda: e.copy(out=self.mr_bc[:, :], in_=PC[:, 0:2]), sC, "gnmr")
            for c in range(ND):
                self.W(e, sT, ("wgw", c))
                self.I(lambda c=c: e.copy(out=self.wgwsb[:, 2 * c:2 * c + 2],
                                          in_=PD[:, 2 * c:2 * c + 2]), sC, ("wgev", c))
                if c >= 1:
                    self.W(e, sV, ("w2c", c - 1))
                    self.I(lambda c=c: e.activation(
                        out=self.z1[:, c - 1, :], in_=self.z1[:, c - 1, :],
                        func=AF.Identity, scale=self.mr_bc[:, 1:2],
                        bias=self.w2sb[:, c - 1:c]), sC, ("z1fx", c - 1))
            self.W(e, sV, ("w2c", ND - 1))
            self.I(lambda: e.activation(
                out=self.z1[:, ND - 1, :], in_=self.z1[:, ND - 1, :],
                func=AF.Identity, scale=self.mr_bc[:, 1:2],
                bias=self.w2sb[:, ND - 1:ND]), sC, ("z1fx", ND - 1))
            for c in range(ND):
                if c >= 2:
                    self.W(e, sT, ("st4", c - 2))
                self.I(lambda c=c: e.activation(out=self.sqbfP[c % 2][:, :],
                                                in_=self.z1[:, c, :],
                                                func=AF.Square), sC, ("z1sq", c))
        if w == "v":
            for c in range(ND):
                par = c % 2
                ps = PA if par == 0 else PB
                self.W(e, sT, ("pe_z1", c))
                self.I(lambda c=c, ps=ps: e.tensor_add(self.z1[:, c, :],
                                                       self.z1[:, c, :], ps[:, :]),
                       sV, ("zadd", c))
                if c == 0:
                    self.W(e, sC, "gnmr")
                    self.I(lambda: e.tensor_mul(self.s2c[:, :], self.mr_bc[:, 0:1],
                                                self.mr_bc[:, 1:2]), sV, "s2c")
                    self.W(e, sV, "s2c")
                self.W(e, sC, ("wgev", c))
                self.I(lambda c=c: e.scalar_tensor_tensor(
                    self.w2sb[:, c:c + 1],
                    self.wgwsb[:, 2 * c + 1:2 * c + 2], self.s2c[:, 0:1],
                    self.wgwsb[:, 2 * c:2 * c + 1],
                    ALU.mult, ALU.add), sV, ("w2c", c))
            for c in range(ND):
                self.W(e, sC, ("z1fx", c))
                if c >= 2:
                    self.W(e, sT, ("st4", c - 2))
                self.I(lambda c=c: e.tensor_copy(self.z1bfP[c % 2][:, :],
                                                 self.z1[:, c, :]), sV, ("z1bf", c))
        self._ln_rows(w, e, PC, PD, self.rows4, self.nmr_bc4, self.rstd_bc4,
                      pref="ln2", wait_tag=("st4", ND - 1))
        if w == "s":
            self._tap_after(e, sC, ("z1fx", ND - 1), "p4z", self.z1[:, :, :])
            self._tap_after(e, sC, ("wgev", ND - 1), "p4w", self.wgwsb[:, :])
            self._tap_after(e, sV, ("w2c", ND - 1), "p4b", self.w2sb[:, :])
            for c in range(ND):
                par = c % 2
                if c == 0:
                    self.W(e, ds["xsp"], "xsp")
                    self.W(e, sT, ("pe_r", NDA - 1))
                    if "p4z" in self.taps:
                        self.W(e, ds["tapq"], "tap_p4z")
                if c >= 2:
                    self.W(e, sV, ("h1", c - 2))
                self.I(lambda c=c, par=par: e.dma_start(
                    out=self.h0rl[par][:, :],
                    in_=self.h0_d.ap()[c * 128:(c + 1) * 128, :]), dps("h0", c), ("h0", c), 16)
        if w == "v":
            self.W(e, sC, "ln2bc")
            for c in range(ND):
                par = c % 2
                self.W(e, dps("h0", c), ("h0", c))
                self.I(lambda c=c, par=par: e.tensor_add(self.tmpa[:, :], self.h0rl[par][:, :],
                                                         self.nmr_bc4[:, :]))
                self.I(lambda c=c: e.tensor_mul(self.tmpb[:, :], self.z1[:, c, :],
                                                self.rstd_bc4[:, :]))
                self.I(lambda c=c: e.tensor_add(self.z1[:, c, :], self.tmpa[:, :],
                                                self.tmpb[:, :]), sV, ("h1", c))
        if w == "a":
            for c in range(ND):
                self.W(e, sV, ("h1", c))
                if c >= 2:
                    self.W(e, sT, ("st4b", c - 2))
                self.I(lambda c=c: e.copy(out=self.z1bfP[c % 2][:, :],
                                          in_=self.z1[:, c, :]), sC, ("h1bf", c))
                self.I(lambda c=c: e.activation(out=self.sqbfP[c % 2][:, :],
                                                in_=self.z1[:, c, :],
                                                func=AF.Square), sC, ("h1sq", c))
        if w == "t":
            for c in range(ND):
                self.W(e, sC, ("h1bf", c))
                self.W(e, sC, ("h1sq", c))
                if c == 0:
                    self.W(e, sC, "ln2bc")
                for h in range(2):
                    self.I(lambda c=c, h=h: e.matmul(
                        PC[0:1, h * 512:(h + 1) * 512], self.onesb[:, 0:1],
                        self.z1bfP[c % 2][:, h * 512:(h + 1) * 512],
                        start=(c == 0), stop=(c == ND - 1)))
                for h in range(2):
                    self.I(lambda c=c, h=h: e.matmul(
                        PD[0:1, h * 512:(h + 1) * 512], self.onesb[:, 0:1],
                        self.sqbfP[c % 2][:, h * 512:(h + 1) * 512],
                        start=(c == 0), stop=(c == ND - 1)), sT, ("st4b", c))
        self._ln_rows(w, e, PC, PD, self.rows4, self.nmr_bc4, self.rstd_bc4,
                      pref="ln3", wait_tag=("st4b", ND - 1))
        if w == "v":
            self.W(e, sC, "ln3bc")
            for c in range(ND):
                self.I(lambda c=c: e.tensor_mul(self.tmpa[:, :], self.z1[:, c, :],
                                                self.rstd_bc4[:, :]))
                self.I(lambda c=c: e.tensor_add(self.abuf[:, c * TC:(c + 1) * TC],
                                                self.tmpa[:, :], self.nmr_bc4[:, :]),
                       sV, ("fin", c))
        if w == "s":
            self.W(e, sV, ("h1", ND - 1))
            self.I(lambda: e.dma_start(out=self.h1_d.ap().rearrange("(c p) t -> p c t", p=128),
                                       in_=self.z1[:, :, :]), ds["h1sp"], "h1sp", 16)
            self._tap_after(e, sV, ("h1", ND - 1), "p4h", self.z1[:, :, :])
            self._tap_after(e, sV, ("fin", ND - 1), "p4f", self.abuf[:, :])

        if self.cut(w, e, 4, self.sV, ("fin", ND - 1), self.z1[:, :, :]):
            return

        self.phase = "p5"
        for half in (0, 1):
            ph = "p5a" if half == 0 else "p5c"
            self.phase = ph
            i0 = half * NIH
            if w == "g":
                for i in range(NIH):
                    par = i % 2
                    if i == 0 and half == 0:
                        self.W(e, sV, ("corr", ND - 1))
                        self.W(e, sC, ("gnsq", ND - 1))
                    if i == 0 and half == 1:
                        self.W(e, sT, ("p5apeu", NIH - 1))
                    if i >= 2:
                        self.W(e, sT, (ph + "peu", i - 2))
                    self.I(lambda i=i, par=par, i0=i0: e.dma_start(
                        out=self.trG[par][:, :, :],
                        in_=self.gw_h[(i0 + i) * 128:(i0 + i + 1) * 128, :].rearrange(
                            "p (k j) -> p k j", j=128)),
                        dps("cg", i), None, 16)
                    self.I(lambda i=i, par=par, i0=i0: e.dma_start(
                        out=self.trU[par][:, :, :],
                        in_=self.uw_h[(i0 + i) * 128:(i0 + i + 1) * 128, :].rearrange(
                            "p (k j) -> p k j", j=128)),
                        dps("cg", i), (ph + "cg", i), 16)
            if w == "t":
                for i in range(NIH):
                    par = i % 2
                    psg = PA if par == 0 else PC
                    psu = PB if par == 0 else PD
                    self.W(e, dps("cg", i), (ph + "cg", i))
                    if i == 1 and half == 0:
                        self.W(e, sC, "ln3bc")
                    if i == 0 and half == 1:
                        self.W(e, sC, ("p5b_z2ev", ND - 1))
                    if i >= 2:
                        self.W(e, sV, (ph + "hdn", i - 2))
                    for h in range(2):
                        for d in range(ND):
                            if i == 0 and half == 0 and h == 0:
                                self.W(e, sV, ("fin", d))
                            last = (d == ND - 1 and h == 1)
                            self.I(lambda psg=psg, i=i, d=d, h=h, par=par: e.matmul(
                                psg[:, h * 512:(h + 1) * 512], self.trG[par][:, d, :],
                                self.abuf[:, d * TC + h * 512: d * TC + (h + 1) * 512],
                                start=(d == 0), stop=(d == ND - 1)),
                                sT if last else None, (ph + "peg", i) if last else None)
                    for h in range(2):
                        for d in range(ND):
                            last = (d == ND - 1 and h == 1)
                            self.I(lambda psu=psu, i=i, d=d, h=h, par=par: e.matmul(
                                psu[:, h * 512:(h + 1) * 512], self.trU[par][:, d, :],
                                self.abuf[:, d * TC + h * 512: d * TC + (h + 1) * 512],
                                start=(d == 0), stop=(d == ND - 1)),
                                sT if last else None, (ph + "peu", i) if last else None)
            if w == "a":
                for i in range(NIH):
                    par = i % 2
                    psg = PA if par == 0 else PC
                    psu = PB if par == 0 else PD
                    self.W(e, sT, (ph + "peg", i))
                    if i >= 1:
                        self.W(e, sV, (ph + "hdn", i - 1))
                    self.I(lambda i=i, psg=psg, i0=i0: e.activation(
                        out=self.sgB[:, :], in_=psg[:, :], func=AF.Silu,
                        scale=self.vecs44[:, i0 + i:i0 + i + 1]), sC, (ph + "sg", i))
                    self.W(e, sT, (ph + "peu", i))
                    self.I(lambda i=i, psu=psu, i0=i0: e.activation(
                        out=self.uuB[:, :], in_=psu[:, :], func=AF.Copy,
                        scale=self.vecs44[:, 44 + i0 + i:44 + i0 + i + 1]), sC, (ph + "uu", i))
            if w == "v":
                for i in range(NIH):
                    self.W(e, sC, (ph + "uu", i))
                    if i == 0 and half == 0:
                        self.W(e, sV, ("fin", ND - 1))
                        self.W(e, ds["h1sp"], "h1sp")
                        self.W(e, sT, "ln3bcmm")
                    if i == 0 and half == 1:
                        self.W(e, sT, ("p5b_pez", ND - 1))
                    self.I(lambda i=i: e.tensor_mul(self.hdn1[:, i, :], self.sgB[:, :],
                                                    self.uuB[:, :]), sV, (ph + "hdn", i))
            if w == "s" and half == 0:
                self._tap_after(e, sV, ("p5ahdn", NIH - 1), "p5a", self.hdn1[:, :, :])

            phd = "p5b_" if half == 0 else "p6_"
            self.phase = phd
            if w == "g":
                for c in range(ND):
                    par = c % 2
                    if c == 0:
                        if half == 0:
                            self.W(e, sV, ("corr", ND - 1))
                        else:
                            self.W(e, sT, ("p5b_pez", ND - 1))
                    if c == 1 and half == 0:
                        self.W(e, sT, ("wgw", ND - 1))
                    if c >= 2:
                        self.W(e, sT, (phd + "pez", c - 2))
                    self.I(lambda c=c, par=par, half=half: e.dma_start(
                        out=self.trD[par][:, :, :],
                        in_=self.dw_h[c * 128:(c + 1) * 128,
                                      half * NIH * 128:(half + 1) * NIH * 128].rearrange(
                            "p (k j) -> p k j", j=128)),
                        dps("cd", c), (phd + "cd", c), 16)
            if w == "t":
                self.W(e, sV, (ph + "hdn", NIH - 1))
                for c in range(ND):
                    par = c % 2
                    ps = PA if par == 0 else PB
                    self.W(e, dps("cd", c), (phd + "cd", c))
                    if c == 0:
                        self.W(e, sC, (ph + "uu", NIH - 1))
                    if c >= 2:
                        if half == 0:
                            self.W(e, sC, (phd + "z2ev", c - 2))
                        else:
                            self.W(e, sV, (phd + "z2ac", c - 2))
                    for h in range(2):
                        for i in range(NIH):
                            last = (i == NIH - 1 and h == 1)
                            self.I(lambda ps=ps, c=c, i=i, h=h, par=par: e.matmul(
                                ps[:, h * 512:(h + 1) * 512], self.trD[par][:, i, :],
                                self.hdn1[:, i, h * 512:(h + 1) * 512],
                                start=(i == 0), stop=(i == NIH - 1)),
                                sT if last else None, (phd + "pez", c) if last else None)
                    if half == 1 and c >= 1:
                        self._st6(e, c - 1)
                if half == 1:
                    self._st6(e, ND - 1)
            if half == 0:
                if w == "a":
                    for c in range(ND):
                        par = c % 2
                        ps = PA if par == 0 else PB
                        self.W(e, sT, (phd + "pez", c))
                        if c == 0:
                            self.W(e, ds["h1sp"], "h1sp")
                        self.I(lambda c=c, ps=ps: e.activation(
                            out=self.z1[:, c, :], in_=ps[:, :], func=AF.Copy,
                            scale=self.vecs16[:, 48 + c:49 + c]), sC, (phd + "z2ev", c))
            else:
                if w == "v":
                    for c in range(ND):
                        par = c % 2
                        ps = PA if par == 0 else PB
                        self.W(e, sT, (phd + "pez", c))
                        self.I(lambda c=c, ps=ps: e.scalar_tensor_tensor(
                            self.z1[:, c, :], ps[:, :], self.vecs16[:, 48 + c:49 + c],
                            self.z1[:, c, :], ALU.mult, ALU.add), sV, (phd + "z2ac", c))
                        if c >= 1:
                            self.W(e, sT, ("st6", c - 1))
                        self.I(lambda c=c: e.tensor_copy(self.z2bf[:, :], self.z1[:, c, :]),
                               sV, ("z2bf", c))
                if w == "a":
                    for c in range(ND):
                        self.W(e, sV, (phd + "z2ac", c))
                        if c >= 1:
                            self.W(e, sT, ("st6", c - 1))
                        self.I(lambda c=c: e.activation(
                            out=self.sqbf6[:, :], in_=self.z1[:, c, :], func=AF.Square),
                            sC, ("z2sq", c))
        if w == "s":
            self._tap_after(e, sC, ("p5b_z2ev", ND - 1), "p5b", self.z1[:, :, :])
            self._tap_after(e, sV, ("p6_z2ac", ND - 1), "p6", self.z1[:, :, :])

        if self.cut(w, e, 6, self.sT, ("st6", ND - 1), self.z1[:, :, :]):
            return

        self.phase = "p7"
        self._ln_rows(w, e, PC, PD, self.rows6, self.nmr_bc7, self.rstd_bc7,
                      pref="ln4", wait_tag=("st6", ND - 1))
        if w == "s":
            def h1load(c):
                if c == 0:
                    self.W(e, ds["h1sp"], "h1sp")
                    self.W(e, sT, ("p5cpeu", NIH - 1))
                    self.W(e, sV, ("p5chdn", NIH - 1))
                self.I(lambda c=c: e.dma_start(
                    out=self.h1rl[c % 4][:, :],
                    in_=self.h1_d.ap()[c * 128:(c + 1) * 128, :]),
                    dps("h1r", c), ("h1r", c), 16)
            for c in range(4):
                h1load(c)
            for c in range(ND):
                if c % 2 == 0:
                    self.W(e, sV, ("out", c))
                    self.I(lambda c=c: e.dma_start(
                        out=self.y_h[c * 128:(c + 1) * 128, :],
                        in_=self.outb[c % 4][:, :]), dps("yq", c), ("yst", c), 16)
                if c + 4 < ND:
                    self.W(e, sV, ("hm", c))
                    h1load(c + 4)
            self.W(e, dps("yq", 0), ("yst", ND - 2))
            for _tn in self.taps:
                self.W(e, ds["tapq"], "tap_" + _tn)
        if w == "g":
            for c in range(1, ND, 2):
                self.W(e, sV, ("out", c))
                self.I(lambda c=c: e.dma_start(
                    out=self.y_h[c * 128:(c + 1) * 128, :],
                    in_=self.outb[c % 4][:, :]), dps("yq", c), ("yst", c), 16)
            self.W(e, dps("yq", 1), ("yst", ND - 1))
        if w == "v":
            self.W(e, sC, "ln4bc")
            for c in range(ND):
                self.I(lambda c=c: e.tensor_mul(self.tmp7a[:, :], self.z1[:, c, :],
                                                self.rstd_bc7[:, :]))
                self.W(e, dps("h1r", c), ("h1r", c))
                self.I(lambda c=c: e.tensor_add(self.tmp7b[:, :], self.h1rl[c % 4][:, :],
                                                self.nmr_bc7[:, :]), sV, ("hm", c))
                if c >= 4:
                    self.W(e, dps("yq", c - 4), ("yst", c - 4))
                self.I(lambda c=c: e.tensor_add(self.outb[c % 4][:, :], self.tmp7a[:, :],
                                                self.tmp7b[:, :]), sV, ("out", c))

    def _st6(self, e, c):
        PC, PD = self.PC, self.PD
        self.W(e, self.sV, ("z2bf", c))
        self.W(e, self.sC, ("z2sq", c))
        if c == 0:
            self.W(e, self.sC, ("p5cuu", NIH - 1))
        for h in range(2):
            self.I(lambda c=c, h=h: e.matmul(
                PC[0:1, h * 512:(h + 1) * 512], self.onesb[:, 0:1],
                self.z2bf[:, h * 512:(h + 1) * 512],
                start=(c == 0), stop=(c == ND - 1)))
        for h in range(2):
            self.I(lambda c=c, h=h: e.matmul(
                PD[0:1, h * 512:(h + 1) * 512], self.onesb[:, 0:1],
                self.sqbf6[:, h * 512:(h + 1) * 512],
                start=(c == 0), stop=(c == ND - 1)), self.sT, ("st6", c))

    def _ln_rows(self, w, e, PSm, PSq, rows, nmr_bc, rstd_bc, pref, wait_tag):
        sT, sV, sC = self.sT, self.sV, self.sC
        denom = float(D)
        if w == "a":
            self.W(e, sT, wait_tag)
            self.I(lambda: e.activation(out=rows[0:1, 0:TC], in_=PSm[0:1, 0:TC],
                                        func=AF.Copy, scale=-1.0 / denom))
            self.I(lambda: e.activation(out=rows[0:1, TC:2 * TC], in_=PSq[0:1, 0:TC],
                                        func=AF.Copy, scale=1.0 / denom), sC, pref + "rows")
        if w == "v":
            self.W(e, sC, pref + "rows")
            self.I(lambda: e.tensor_mul(rows[0:1, 2 * TC:3 * TC], rows[0:1, 0:TC],
                                        rows[0:1, 0:TC]))
            self.I(lambda: e.tensor_sub(rows[0:1, TC:2 * TC], rows[0:1, TC:2 * TC],
                                        rows[0:1, 2 * TC:3 * TC]))
            self.I(lambda: e.tensor_scalar_add(rows[0:1, TC:2 * TC], rows[0:1, TC:2 * TC],
                                               EPS), sV, pref + "var")
        if w == "a":
            self.W(e, sV, pref + "var")
            self.I(lambda: e.activation(out=rows[0:1, TC:2 * TC], in_=rows[0:1, TC:2 * TC],
                                        func=AF.Ln))
            self.I(lambda: e.activation(out=rows[0:1, TC:2 * TC], in_=rows[0:1, TC:2 * TC],
                                        func=AF.Exp, scale=-0.5), sC, pref + "rstd")
        if w == "v":
            self.W(e, sC, pref + "rstd")
            self.I(lambda: e.tensor_mul(rows[0:1, 0:TC], rows[0:1, 0:TC],
                                        rows[0:1, TC:2 * TC]), sV, pref + "nmr")
        if w == "t":
            self.W(e, sV, pref + "nmr")
            self.W(e, sV, "o1r")
            for h in range(2):
                self.I(lambda h=h: e.matmul(PSm[:, h * 512:(h + 1) * 512], self.ones1r[:, :],
                                            rows[0:1, h * 512:(h + 1) * 512],
                                            start=True, stop=True))
            for h in range(2):
                self.I(lambda h=h: e.matmul(PSq[:, h * 512:(h + 1) * 512], self.ones1r[:, :],
                                            rows[0:1, TC + h * 512:TC + (h + 1) * 512],
                                            start=True, stop=True), sT, pref + "bcmm")
        if w == "a":
            self.W(e, self.sT, pref + "bcmm")
            self.I(lambda: e.copy(out=nmr_bc[:, :], in_=PSm[:, :]))
            self.I(lambda: e.copy(out=rstd_bc[:, :], in_=PSq[:, :]), sC, pref + "bc")

    def _wo_load(self, e, c):
        dps = self.ps
        if c < 3:
            self.W(e, self.sV, ("scan", NDA - 1))
            self.I(lambda c=c: e.dma_start(
                out=self.trWoA[c][:, :, :],
                in_=self.wo_h[c * 128:(c + 1) * 128, :].rearrange(
                    "p (k j) -> p k j", j=128)),
                dps("co", c), ("co", c), 16)
            return
        if c == 3:
            self.W(e, self.sV, ("corr", 11))
        if c >= 5:
            self.W(e, self.sT, ("wgw", c - 2))
        self.I(lambda c=c: e.dma_start(
            out=self.trWo[c % 2][:, :, :],
            in_=self.wo_h[c * 128:(c + 1) * 128, :].rearrange(
                "p (k j) -> p k j", j=128)),
            dps("co", c), ("co", c), 16)

    def _carry_stage(self, e, g):
        self.W(e, self.sV, ("gt", g * 4 + 3))
        self.I(lambda g=g: e.dma_start(out=self.cc_in[g].ap()[:, :], in_=self.gt[:, :]),
               self.dsem["gts"], ("gts", g), 16)
        self.W(e, self.dsem["gts"], ("gts", g))
        if not NO_CC:
            self.I(lambda g=g: e.collective_compute(
                "AllGather", ALU.bypass, replica_groups=[list(range(NCORE))],
                ins=[self.cc_in[g].ap().opt()], outs=[self.cc_out[g].ap().opt()]),
                self.sCC, ("cc", g))

    def _carry_back(self, e, g):
        if NO_CC:
            if g >= 1:
                self.W(e, self.sV, ("cin", g - 1))
            self.I(lambda g=g: e.memset(self.gath[:, :], 0.0),
                   self.dsem["gtb"], ("gtb", g), 16)
            return
        self.W(e, self.sCC, ("cc", g))
        if g >= 1:
            self.W(e, self.sV, ("cin", g - 1))
        self.I(lambda g=g: e.dma_start(
            out=self.gath[:, :].rearrange("p (r j) -> p r j", j=4),
            in_=self.cc_out[g].ap().rearrange("(r p) j -> p r j", p=128)),
            self.dsem["gtb"], ("gtb", g), 16)

    def _tap(self, e, name, view):
        if name not in self.taps:
            return
        self.I(lambda: e.dma_start(out=self.taps[name], in_=view), self.dsem["tapq"], "tap_" + name, 16)

    def _tap_after(self, e, sem, tag, name, view):
        if name not in self.taps:
            return
        self.W(e, sem, tag)
        v = view.rearrange("p a b -> p (a b)") if len(view.shape) == 3 else view
        self.I(lambda: e.dma_start(out=self.taps[name], in_=v), self.dsem["tapq"], "tap_" + name, 16)


def _prep_inputs(L0_attn_x, L0_attn_Wr, L0_attn_Wk, L0_attn_Wv, L0_attn_Wo,
                 L0_attn_decay, L0_attn_gn_weight, L0_attn_gn_bias,
                 L0_ffn_gate_w, L0_ffn_up_w, L0_ffn_down_w,
                 L0_ffn_gate_s, L0_ffn_up_s, L0_ffn_down_s):
    xT = np.ascontiguousarray(np.asarray(L0_attn_x, np.float32)[0].T)

    def tile_lhsT(v):
        w = np.asarray(v, np.float32)
        J2, K2 = w.shape
        return np.ascontiguousarray(
            w.reshape(J2 // 128, 128, K2 // 128, 128).transpose(0, 3, 2, 1)
            .reshape(J2, K2))

    wr = tile_lhsT(L0_attn_Wr)
    wk = tile_lhsT(L0_attn_Wk)
    wv = tile_lhsT(L0_attn_Wv)
    wo = tile_lhsT(L0_attn_Wo)
    gw = tile_lhsT(L0_ffn_gate_w)
    uw = tile_lhsT(L0_ffn_up_w)
    dw = tile_lhsT(L0_ffn_down_w)

    def cols16(v):
        return np.ascontiguousarray(np.asarray(v, np.float32).reshape(16, 128).T)

    vecs16 = np.ascontiguousarray(np.concatenate(
        [cols16(L0_attn_decay), cols16(L0_attn_gn_weight),
         cols16(L0_attn_gn_bias), cols16(L0_ffn_down_s)], axis=1))

    def cols44(v):
        return np.asarray(v, np.float32).reshape(44, 128).T

    vecs44 = np.ascontiguousarray(
        np.concatenate([cols44(L0_ffn_gate_s), cols44(L0_ffn_up_s)], axis=1))

    in_maps = []
    for core in range(NCORE):
        iota = np.broadcast_to(
            np.arange(core * TC, (core + 1) * TC, dtype=np.float32)[None, :],
            (128, TC)).copy()
        cmask = np.zeros((128, 8, 4), np.float32)
        cmask[:, :core, :] = 1.0
        in_maps.append({
            "x": np.ascontiguousarray(xT[:, core * TC:(core + 1) * TC]),
            "wr": wr, "wk": wk, "wv": wv, "wo": wo,
            "gw": gw, "uw": uw, "dw": dw,
            "vecs16": vecs16, "vecs44": vecs44,
            "iota": iota, "cmask": np.ascontiguousarray(cmask.reshape(128, 32)),
        })
    return in_maps


_CACHED = {}


def kernel(**inputs):
    in_maps = _prep_inputs(**inputs)
    key = ("nc", TAPS, LIMIT, NO_CC, NDA, DBG0)
    if key not in _CACHED:
        _CACHED.clear()
        _CACHED[key] = Builder().build()
    nc = _CACHED[key]
    res = run_bass_kernel_spmd(nc, in_maps, core_ids=list(range(NCORE)))
    out = np.empty((T, D), np.float32)
    for core in range(NCORE):
        out[core * TC:(core + 1) * TC, :] = res.results[core]["y"].T
    kernel.last_results = res
    return out.reshape(1, T, D)
